# version 36
# baseline (speedup 1.0000x reference)
import numpy as np
from contextlib import ExitStack

import concourse.bass as bass
import concourse.mybir as mybir
from concourse.ap import AP as RawAP
from concourse.bass_utils import run_bass_kernel_spmd

F32 = mybir.dt.float32
BF16 = mybir.dt.bfloat16
AF = mybir.ActivationFunctionType
ALU = mybir.AluOpType

D = 1024
DI = 2048
S = 2048
N = 16
DTR = 64
BLOC = 2
CS = 512
SH = 1024
NCORE = 8
NDC = DI // 128
NMC = D // 128
EPS = 1e-5

_EMITTERS = {
    "tensor_copy", "tensor_add", "tensor_sub", "tensor_mul", "tensor_max",
    "tensor_scalar", "tensor_scalar_add", "tensor_scalar_sub",
    "tensor_scalar_mul", "tensor_scalar_max", "tensor_scalar_min",
    "tensor_single_scalar", "tensor_tensor", "scalar_tensor_tensor",
    "tensor_tensor_scan", "tensor_reduce", "reduce_max", "reduce_sum",
    "reciprocal", "memset", "activation", "matmul", "dma_start",
    "tensor_relu",
}
ENGS = ("sp", "pe", "act", "dve", "pool")
_ENG_ATTR = {"sp": "sync", "pe": "tensor", "act": "scalar",
             "dve": "vector", "pool": "gpsimd"}
NQ = 6


class _CountProxy:
    def __init__(self):
        self.n = 0

    def __getattr__(self, name):
        if name in _EMITTERS:
            def f(*a, **kw):
                self.n += 16 if name == "dma_start" else 1
                return self
            return f
        raise AttributeError(name)

    def then_inc(self, *a):
        return self


class _EmitProxy:
    def __init__(self, eng, sem, start, serial, mmg):
        self._eng = eng
        self._sem = sem
        self._cnt = start
        self._start = start
        self._serial = serial
        self._mmg = mmg

    def __getattr__(self, name):
        if name not in _EMITTERS:
            raise AttributeError(name)
        real = getattr(self._eng, name)

        def f(*a, **kw):
            if (self._serial and self._cnt > self._start
                    and not (name == "matmul" and self._mmg)):
                self._eng.wait_ge(self._sem, self._cnt)
            ins = real(*a, **kw)
            ins.then_inc(self._sem, 16 if name == "dma_start" else 1)
            self._cnt += 16 if name == "dma_start" else 1
            return ins
        return f


class _Step:
    __slots__ = ("eng", "sem_name", "fn", "reads", "writes", "serial",
                 "mmg", "waits", "start", "end", "n")

    def __init__(self, eng, sem_name, fn, reads, writes, serial, mmg):
        self.eng, self.sem_name, self.fn = eng, sem_name, fn
        self.reads, self.writes = list(reads), list(writes)
        self.serial, self.mmg = serial, mmg


class DF:
    def __init__(self, nc, top):
        self.nc = nc
        names = list(ENGS) + [f"q{i}" for i in range(NQ)]
        self.sems = {e: top.enter_context(nc.semaphore("prog_" + e))
                     for e in names}
        self.steps = []
        self._qi = 0
        self.cnt = {s: 0 for s in self.sems}
        self.wm = {e: {s: 0 for s in self.sems} for e in ENGS}
        self.writer = {}
        self.readers = {}
        self.pending = {}

    def barrier(self):
        snap = dict(self.cnt)
        for e in ENGS:
            self.pending[e] = snap

    def step(self, eng, fn, reads=(), writes=(), serial=True, mmg=False):
        self.steps.append(_Step(eng, eng, fn, reads, writes, serial, mmg))

    def dma(self, pairs, reads=(), writes=(), **kw):
        pairs = list(pairs)
        qi = self._qi
        self._qi = (self._qi + 1) % NQ

        def fn(e):
            for dst, src in pairs:
                e.dma_start(out=dst, in_=src, **kw)
        self.steps.append(_Step("sp", f"q{qi}", fn, reads,
                                list(writes) + [("__dmaq__", qi)], False,
                                False))

    def flush(self, block):
        cnt, wm, writer, readers = self.cnt, self.wm, self.writer, self.readers
        for st in self.steps:
            c = _CountProxy()
            st.fn(c)
            st.n = c.n
            waits = {}

            def need(dep, _w=waits, _e=st.eng):
                if dep is None:
                    return
                sname, val = dep
                if val > wm[_e][sname]:
                    _w[sname] = max(_w.get(sname, 0), val)
            if st.eng in self.pending:
                for sname, val in self.pending.pop(st.eng).items():
                    need((sname, val))
            for k in st.reads:
                need(writer.get(k))
            for k in st.writes:
                need(writer.get(k))
                for r in readers.get(k, ()):
                    need(r)
            st.waits = sorted(waits.items())
            for sname, val in st.waits:
                wm[st.eng][sname] = val
            st.start = cnt[st.sem_name]
            cnt[st.sem_name] += st.n
            st.end = cnt[st.sem_name]
            for k in st.writes:
                writer[k] = (st.sem_name, st.end)
                readers[k] = []
            for k in st.reads:
                readers.setdefault(k, []).append((st.sem_name, st.end))

        by_eng = {}
        for st in self.steps:
            by_eng.setdefault(st.eng, []).append(st)
        for eng_name in ENGS:
            sts = by_eng.get(eng_name, [])
            if not sts:
                continue

            def emit(e, sts=sts):
                for st in sts:
                    for sname, val in st.waits:
                        e.wait_ge(self.sems[sname], val)
                    p = _EmitProxy(e, self.sems[st.sem_name], st.start,
                                   st.serial, st.mmg)
                    st.fn(p)
                    assert p._cnt == st.end, (st.eng, st.n, p._cnt - st.start)
            getattr(block, _ENG_ATTR[eng_name])(emit)
        self.steps = []

    def finish(self, block):
        def emit(e):
            for sname, val in self.cnt.items():
                if val > 0:
                    e.wait_ge(self.sems[sname], val)
        block.sync(emit)


def bcast(src_ap, parts, n):
    return RawAP(src_ap.tensor, src_ap.offset, [[0, parts], [1, n]])


def ap3(t, offset, dims):
    base = t[:, :]
    return RawAP(base.tensor, base.offset + offset, dims)


def build_nc():
    nc = bass.Bass("TRN2", target_bir_lowering=False, debug=False)

    x_fm = nc.dram_tensor("x_fm", [BLOC, D, S], F32, kind="ExternalInput").ap()
    w_in = nc.dram_tensor("w_in", [D, 2 * DI], F32, kind="ExternalInput").ap()
    w_xp = nc.dram_tensor("w_xp", [DI, 96], F32, kind="ExternalInput").ap()
    w_dt = nc.dram_tensor("w_dt", [DTR, DI], F32, kind="ExternalInput").ap()
    w_out = nc.dram_tensor("w_out", [DI, D], F32, kind="ExternalInput").ap()
    w_fg = nc.dram_tensor("w_fg", [D, D], F32, kind="ExternalInput").ap()
    w_wg = nc.dram_tensor("w_wg", [D, 8], F32, kind="ExternalInput").ap()
    w_we = nc.dram_tensor("w_we", [D, D], F32, kind="ExternalInput").ap()
    conv_w = nc.dram_tensor("conv_w", [DI, 4], F32, kind="ExternalInput").ap()
    conv_b = nc.dram_tensor("conv_b", [DI], F32, kind="ExternalInput").ap()
    dt_b = nc.dram_tensor("dt_b", [DI], F32, kind="ExternalInput").ap()
    a_log = nc.dram_tensor("a_log", [DI, N], F32, kind="ExternalInput").ap()
    d_vec = nc.dram_tensor("d_vec", [DI], F32, kind="ExternalInput").ap()
    fg_b = nc.dram_tensor("fg_b", [D], F32, kind="ExternalInput").ap()
    bg = nc.dram_tensor("bg", [8], F32, kind="ExternalInput").ap()
    be = nc.dram_tensor("be", [D], F32, kind="ExternalInput").ap()
    out_fm = nc.dram_tensor("out_fm", [BLOC, D, S], F32,
                            kind="ExternalOutput").ap()

    u_hbm = nc.dram_tensor("u_hbm", [BLOC, NDC, 128, S], BF16, kind="Internal").ap()
    z_hbm = nc.dram_tensor("z_hbm", [BLOC, NDC, 128, S], BF16, kind="Internal").ap()
    y_hbm = nc.dram_tensor("y_hbm", [BLOC, NDC, 128, S], BF16, kind="Internal").ap()
    xdblb_hbm = nc.dram_tensor("xdblb_hbm", [32, BLOC * S], BF16, kind="Internal").ap()
    rows_hbm = nc.dram_tensor("rows_hbm", [4, CS], F32, kind="Internal").ap()
    ident_hbm = nc.dram_tensor("ident_hbm", [128, 128], BF16, kind="Internal").ap()

    with ExitStack() as top:
        block = top.enter_context(nc.Block())
        df = DF(nc, top)
        sb = lambda name, shape, dt=F32: top.enter_context(
            nc.sbuf_tensor(name, shape, dt))

        warena = sb("warena", [128, 32832], BF16)
        fgbc = sb("fgbc", [128, NMC], F32)
        bec = sb("bec", [128, NMC], F32)
        bgrep = sb("bgrep", [128, 8], F32)
        ones32 = sb("ones32", [128, 1], F32)
        ident = sb("ident", [128, 128], BF16)
        ab = top.enter_context(ExitStack())
        sbab = lambda name, shape, dt=F32: ab.enter_context(
            nc.sbuf_tensor(name, shape, dt))
        wxp = sbab("wxp", [128, NDC * 96], BF16)
        wdt = sbab("wdt", [64, DI], BF16)
        xdbl_b = sbab("xdbl_b", [96, BLOC * S], BF16)
        acol = sbab("acol", [128, NDC * N], F32)
        cw = sbab("cw", [128, NDC * 4], F32)
        cb = sbab("cb", [128, NDC], F32)
        dtbc = sbab("dtbc", [128, NDC], F32)
        dcol = sbab("dcol", [128, NDC], F32)
        tails = sbab("tails", [128, NDC * 3], F32)
        hstate = sbab("hstate", [128, NDC * N], F32)

        w_in_ap = lambda kc, mc: warena[:, kc * 4096 + mc * 128:
                                        kc * 4096 + (mc + 1) * 128]
        w_out_ap = lambda dc, mc: warena[:, dc * 1024 + mc * 128:
                                         dc * 1024 + (mc + 1) * 128]
        w_fg_ap = lambda kc, mc: warena[:, 16384 + kc * 1024 + mc * 128:
                                        16384 + kc * 1024 + (mc + 1) * 128]
        w_we_ap = lambda kc, mc: warena[:, 24576 + kc * 1024 + mc * 128:
                                        24576 + kc * 1024 + (mc + 1) * 128]
        w_wg_ap = lambda kc: warena[:, 32768 + kc * 8: 32768 + (kc + 1) * 8]
        brep_ap = lambda n: warena[:, n * SH:(n + 1) * SH]
        crep_ap = lambda n: warena[:, 16384 + n * SH: 16384 + (n + 1) * SH]

        with ExitStack() as es:
            wstg = [es.enter_context(nc.sbuf_tensor(f"wstage{i}", [128, 2048],
                                                    F32)) for i in range(2)]
            wstage = wstg[0]
            vstg = [es.enter_context(nc.sbuf_tensor(f"vstage{i}", [128, 96],
                                                    F32)) for i in range(2)]
            vstage = vstg[0]

            onebf = es.enter_context(nc.sbuf_tensor("onebf", [128, 1], BF16))

            def prep_const(e):
                e.memset(ones32[:, :], 1.0)
                e.memset(ident[:, :], 0.0)
                e.memset(onebf[:, :], 1.0)
            df.step("dve", prep_const,
                    writes=[("ones",), ("ident",), ("onebf",)], serial=False)
            df.dma([(ident_hbm, ident[:, :])],
                   reads=[("ident",)], writes=[("idh",)])
            diag = RawAP(ident_hbm.tensor, ident_hbm.offset, [[129, 128], [1, 1]])
            df.dma([(diag, onebf[:, :])], reads=[("onebf",)], writes=[("idh",)],
                   allow_slow_non_contiguous=True)
            df.dma([(ident[:, :], ident_hbm)],
                   reads=[("idh",)], writes=[("identd",)])

            for hr in range(16):
                kc, h = divmod(hr, 2)
                wp = hr % 2
                df.dma([(wstg[wp][:, :],
                         w_in[kc * 128:(kc + 1) * 128,
                              h * 2048:(h + 1) * 2048])],
                       writes=[("wstage", wp)])
                df.step("dve", lambda e, kc=kc, h=h, wp=wp: e.tensor_copy(
                    warena[:, kc * 4096 + h * 2048:kc * 4096 + (h + 1) * 2048],
                    wstg[wp][:, :]),
                    reads=[("wstage", wp)],
                    writes=[("wa", 4 * kc + 2 * h + j) for j in range(2)])
            for dc in range(NDC):
                vp = dc % 2
                df.dma([(vstg[vp][:, 0:96], w_xp[dc * 128:(dc + 1) * 128, :])],
                       writes=[("vstage", vp)])
                df.step("dve", lambda e, dc=dc, vp=vp: e.tensor_copy(
                    wxp[:, dc * 96:(dc + 1) * 96], vstg[vp][:, 0:96]),
                    reads=[("vstage", vp)], writes=[("wxp", dc)])
            df.dma([(wstage[0:64, 0:DI], w_dt)], writes=[("wstage", 0)])
            df.step("dve", lambda e: e.tensor_copy(wdt[:, :], wstage[0:64, 0:DI]),
                    reads=[("wstage", 0)], writes=[("wdt",)])
            for dc in range(NDC):
                df.dma([(vstage[:, 0:16], a_log[dc * 128:(dc + 1) * 128, :]),
                        (cw[:, dc * 4:(dc + 1) * 4], conv_w[dc * 128:(dc + 1) * 128, :]),
                        (cb[:, dc:dc + 1], conv_b[dc * 128:(dc + 1) * 128].unsqueeze(1)),
                        (dtbc[:, dc:dc + 1], dt_b[dc * 128:(dc + 1) * 128].unsqueeze(1)),
                        (dcol[:, dc:dc + 1], d_vec[dc * 128:(dc + 1) * 128].unsqueeze(1))],
                       writes=[("vstage", 0), ("cvecs", dc)])
                df.step("act", lambda e, dc=dc: e.activation(
                    out=acol[:, dc * 16:(dc + 1) * 16], in_=vstage[:, 0:16],
                    func=AF.Exp), reads=[("vstage", 0)], writes=[("acole", dc)])
                df.step("dve", lambda e, dc=dc: e.tensor_scalar_mul(
                    acol[:, dc * 16:(dc + 1) * 16],
                    acol[:, dc * 16:(dc + 1) * 16], -1.0),
                    reads=[("acole", dc)], writes=[("acol", dc)])
            dmas = []
            for mc in range(NMC):
                dmas.append((fgbc[:, mc:mc + 1], fg_b[mc * 128:(mc + 1) * 128].unsqueeze(1)))
                dmas.append((bec[:, mc:mc + 1], be[mc * 128:(mc + 1) * 128].unsqueeze(1)))
            dmas.append((bgrep[:, :], bcast(bg.unsqueeze(0), 128, 8)))
            df.dma(dmas, writes=[("cvecs2",)])

            xs = [es.enter_context(nc.sbuf_tensor(f"xs{i}", [128, 8 * CS], F32))
                  for i in range(2)]
            xbb = [es.enter_context(nc.sbuf_tensor(f"xb{i}", [128, 8 * CS], BF16))
                   for i in range(2)]
            uball = [es.enter_context(nc.sbuf_tensor("ub0", [128, NDC * CS], BF16))]
            z_all = es.enter_context(nc.sbuf_tensor("z_all", [128, NDC * CS], BF16))
            xi_pad = [es.enter_context(nc.sbuf_tensor(f"xi_pad{i}", [128, CS + 3], BF16))
                      for i in range(2)]
            cacc = [es.enter_context(nc.sbuf_tensor(f"cacc{i}", [128, CS], BF16))
                    for i in range(2)]
            tailsb = es.enter_context(nc.sbuf_tensor("tailsb", [128, NDC * 3], BF16))
            xdbl = es.enter_context(nc.sbuf_tensor("xdbl", [96, BLOC * S], F32))
            psA = [es.enter_context(nc.psum_tensor(f"psA{i}", [128, CS], F32))
                   for i in range(4)]
            psX = es.enter_context(nc.psum_tensor("psX", [96, CS], F32))

            def load_xs(it):
                b, sc = divmod(it, S // CS)
                par = it % 2
                xsb = xs[par][:, :]
                xfb = x_fm[b, :, sc * CS:sc * CS + CS]
                df.dma([(RawAP(xsb.tensor, xsb.offset,
                               [[8 * CS, 128], [CS, 8], [1, CS]]),
                         RawAP(xfb.tensor, xfb.offset,
                               [[S, 128], [128 * S, 8], [1, CS]]))],
                       writes=[("xs", par)])
            load_xs(0)
            for b in range(BLOC):
                df.step("dve", lambda e: e.memset(tailsb[:, :], 0.0),
                        writes=[("tails", dc) for dc in range(NDC)],
                        serial=False)
                for sc in range(S // CS):
                    s0 = sc * CS
                    it = b * (S // CS) + sc
                    par = it % 2
                    df.step("dve", lambda e, par=par: e.tensor_copy(
                        xbb[par][:, :], xs[par][:, :]),
                        reads=[("xs", par)], writes=[("xb", par)])
                    if it + 1 < BLOC * (S // CS):
                        load_xs(it + 1)
                    def emit_xicopy(dc):
                        pp = dc % 4
                        xp = dc % 2
                        df.step("act", lambda e, pp=pp, xp=xp: e.activation(
                            out=xi_pad[xp][:, 3:3 + CS], in_=psA[pp][:, :],
                            func=AF.Copy), reads=[("psA", pp)],
                            writes=[("xim", xp)])

                    def emit_conv(dc):
                        xp = dc % 2

                        def conv_fn(e, dc=dc, xp=xp):
                            xip = xi_pad[xp]
                            cc = cacc[xp]
                            e.tensor_copy(xip[:, 0:3],
                                          tailsb[:, dc * 3:(dc + 1) * 3])
                            e.tensor_copy(tailsb[:, dc * 3:(dc + 1) * 3],
                                          xip[:, CS:CS + 3])
                            e.tensor_scalar(cc[:, :], xip[:, 0:CS],
                                            cw[:, dc * 4:dc * 4 + 1], None,
                                            ALU.mult)
                            for k in (1, 2, 3):
                                e.scalar_tensor_tensor(
                                    cc[:, :], xip[:, k:k + CS],
                                    cw[:, dc * 4 + k:dc * 4 + k + 1],
                                    cc[:, :], ALU.mult, ALU.add)
                        df.step("dve", conv_fn,
                                reads=[("xim", xp), ("tails", dc), ("cvecs", dc)],
                                writes=[("cacc", xp), ("tails", dc)])
                        df.step("act", lambda e, dc=dc, xp=xp: e.activation(
                            out=uball[0][:, dc * CS:(dc + 1) * CS],
                            in_=cacc[xp][:, :], func=AF.Silu,
                            bias=cb[:, dc:dc + 1]),
                            reads=[("cacc", xp), ("cvecs", dc)],
                            writes=[("ub", 0, dc)])

                    for mc in range(32):
                        pp = mc % 4

                        def mm_in(e, mc=mc, pp=pp, par=par):
                            for kc in range(8):
                                e.matmul(psA[pp][:, :], w_in_ap(kc, mc),
                                         xbb[par][:, kc * CS:(kc + 1) * CS],
                                         start=(kc == 0), stop=(kc == 7))
                        df.step("pe", mm_in, mmg=True,
                                reads=[("xb", par)] + [("wa", 4 * kc + mc // 8) for kc in range(8)],
                                writes=[("psA", pp)])
                        if mc < 16:
                            emit_xicopy(mc)
                            if mc >= 1:
                                emit_conv(mc - 1)
                            if mc == 15:
                                emit_conv(15)
                        else:
                            zc = mc - 16
                            df.step("act", lambda e, pp=pp, zc=zc: e.activation(
                                out=z_all[:, zc * CS:(zc + 1) * CS],
                                in_=psA[pp][:, :], func=AF.Silu),
                                reads=[("psA", pp)], writes=[("za", zc)])


                    uab = uball[0][:, :]
                    uhb = u_hbm[b, 0, :, s0:s0 + CS]
                    df.dma([(RawAP(uhb.tensor, uhb.offset + d0 * 128 * S,
                                   [[S, 128], [128 * S, 4], [1, CS]]),
                             RawAP(uab.tensor, uab.offset + d0 * CS,
                                   [[NDC * CS, 128], [CS, 4], [1, CS]]))
                            for d0 in range(0, NDC, 4)],
                           reads=[("ub", 0, dc) for dc in range(NDC)],
                           writes=[("uh", b, dc, sc) for dc in range(NDC)])

                    def mm_xp(e, par=par):
                        for dc in range(NDC):
                            e.matmul(psX[:, :], wxp[:, dc * 96:(dc + 1) * 96],
                                     uball[0][:, dc * CS:(dc + 1) * CS],
                                     start=(dc == 0), stop=(dc == NDC - 1))
                    df.step("pe", mm_xp, mmg=True,
                            reads=[("ub", 0, dc) for dc in range(NDC)] +
                                  [("wxp", dc) for dc in range(NDC)],
                            writes=[("psX",)])
                    zab = z_all[:, :]
                    zhb = z_hbm[b, 0, :, s0:s0 + CS]
                    df.dma([(RawAP(zhb.tensor, zhb.offset + d0 * 128 * S,
                                   [[S, 128], [128 * S, 4], [1, CS]]),
                             RawAP(zab.tensor, zab.offset + d0 * CS,
                                   [[NDC * CS, 128], [CS, 4], [1, CS]]))
                            for d0 in range(0, NDC, 4)],
                           reads=[("za", zc) for zc in range(NDC)],
                           writes=[("zh", b, zc, sc) for zc in range(NDC)])
                    df.step("dve", lambda e, b=b, s0=s0: e.tensor_copy(
                        xdbl[:, b * S + s0: b * S + s0 + CS], psX[:, :]),
                        reads=[("psX",)], writes=[("xdbl", b, s0)])
            df.step("dve", lambda e: e.tensor_copy(xdbl_b[:, :], xdbl[:, :]),
                    reads=[("xdbl", b, sc * CS) for b in range(BLOC)
                           for sc in range(S // CS)],
                    writes=[("xdblb",)])
            df.dma([(xdblb_hbm[:, :], xdbl_b[64:96, :])],
                   reads=[("xdblb",)], writes=[("xdblh",)])
            df.flush(block)
        df.barrier()

        with ExitStack() as es:
            sbB = lambda name, shape, dt=F32: es.enter_context(
                nc.sbuf_tensor(name, shape, dt))
            GN = 4
            NG = N // GN
            SEG = SH + 1
            dabg = [sbB(f"dabg{i}", [128, GN * SEG]) for i in range(2)]
            dubg = sbB("dubg", [128, GN * SEG], BF16)
            hbg = [sbB(f"hbg{i}", [128, GN * SEG], BF16) for i in range(2)]
            prodg = [sbB(f"prodg{i}", [128, GN * SH], BF16) for i in range(2)]
            dt32 = [sbB(f"dt32{i}", [128, SH]) for i in range(2)]
            rawb = sbB("rawb", [128, SH])
            tmpb = sbB("tmpb", [128, SH])
            dub16 = [sbB(f"dub16{i}", [128, SH], BF16) for i in range(2)]
            u_sb = [sbB(f"usb{i}", [128, SH], BF16) for i in range(2)]
            z_sb = [sbB(f"zsb{i}", [128, SH], BF16) for i in range(2)]
            y32 = sbB("y32", [128, SH])
            ybf = [sbB(f"ybf{i}", [128, SH], BF16) for i in range(2)]
            psD = es.enter_context(nc.psum_tensor("psD", [128, SH], F32))
            psY = es.enter_context(nc.psum_tensor("psY", [128, SH], F32))

            def zero_dabg(e):
                e.memset(dabg[0][:, :], 0.0)
                e.memset(dabg[1][:, :], 0.0)
            df.step("dve", zero_dabg, writes=[("dabg", 0), ("dabg", 1)],
                    serial=False)

            for b in range(BLOC):
                for sh in range(S // SH):
                    h0 = sh * SH
                    col0 = b * S + h0
                    wab = warena[:, :]
                    for r0 in (0, 16, 4, 20, 8, 24, 12, 28):
                        xh = xdblb_hbm[r0, col0:col0 + SH]
                        df.dma([(RawAP(wab.tensor, wab.offset + r0 * SH,
                                       [[32832, 128], [SH, 4], [1, SH]]),
                                 RawAP(xh.tensor, xh.offset,
                                       [[0, 128], [BLOC * S, 4], [1, SH]]))],
                               reads=[("xdblh",)],
                               writes=[("wa", r0 + j) for j in range(4)])
                    
                    def emit_glue(dc):
                        dpar = dc % 2

                        def mm_dt(e, dc=dc, col0=col0):
                            for hh in (0, 512):
                                e.matmul(psD[:, hh:hh + 512],
                                         wdt[:, dc * 128:(dc + 1) * 128],
                                         xdbl_b[0:64, col0 + hh:col0 + hh + 512],
                                         start=True, stop=True)
                        df.step("pe", mm_dt, mmg=True,
                                reads=[("xdblb",), ("wdt",)], writes=[("psD",)])

                        df.step("act", lambda e, dc=dc: e.activation(
                            out=rawb[:, :], in_=psD[:, :], func=AF.Identity,
                            bias=dtbc[:, dc:dc + 1]),
                            reads=[("psD",), ("cvecs", dc)], writes=[("rawb",)])
                        df.step("dve", lambda e: e.tensor_scalar(
                            tmpb[:, :], rawb[:, :], 30.0, None, ALU.min),
                            reads=[("rawb",)], writes=[("tmpb",)])

                        def sp2(e):
                            e.activation(out=tmpb[:, :], in_=tmpb[:, :], func=AF.Exp)
                            e.activation(out=tmpb[:, :], in_=tmpb[:, :], func=AF.Ln,
                                         bias=1.0)
                        df.step("act", sp2, reads=[("tmpb",)], writes=[("tmpb",)])
                        df.step("dve", lambda e, dpar=dpar: e.tensor_max(
                            dt32[dpar][:, :], tmpb[:, :], rawb[:, :]),
                            reads=[("tmpb",), ("rawb",)], writes=[("dt", dpar)])
                        df.dma([(u_sb[dpar][:, :], u_hbm[b, dc, :, h0:h0 + SH]),
                                (z_sb[dpar][:, :], z_hbm[b, dc, :, h0:h0 + SH])],
                               reads=[("uh", b, dc, sh * 2), ("uh", b, dc, sh * 2 + 1),
                                      ("zh", b, dc, sh * 2), ("zh", b, dc, sh * 2 + 1)],
                               writes=[("usb", dpar), ("zsb", dpar)])
                        df.step("dve", lambda e, dpar=dpar: e.tensor_tensor(
                            dub16[dpar][:, :], dt32[dpar][:, :], u_sb[dpar][:, :],
                            ALU.mult),
                            reads=[("dt", dpar), ("usb", dpar)], writes=[("du", dpar)])


                    def emit_gloop(dc):
                        dpar = dc % 2
                        for g in range(NG):
                            gpar = g % 2
                            n0 = g * GN

                            def dabf(e, dc=dc, n0=n0, gpar=gpar, dpar=dpar):
                                for j in range(GN):
                                    e.activation(
                                        out=dabg[gpar][:, j * SEG + 1:
                                                       j * SEG + 1 + SH],
                                        in_=dt32[dpar][:, :], func=AF.Exp,
                                        scale=acol[:, dc * 16 + n0 + j:
                                                   dc * 16 + n0 + j + 1])
                            df.step("act", dabf, serial=False,
                                    reads=[("dt", dpar), ("acol", dc)],
                                    writes=[("dabg", gpar)])

                            def dubf(e, dc=dc, g=g, n0=n0, sh=sh, dpar=dpar):
                                icols = ap3(dubg, 0,
                                            [[GN * SEG, 128], [SEG, GN], [1, 1]])
                                if sh == 0:
                                    e.memset(icols, 0.0)
                                else:
                                    e.tensor_copy(
                                        icols, hstate[:, dc * 16 + n0:
                                                      dc * 16 + n0 + GN])
                                e.tensor_tensor(
                                    ap3(dubg, 1, [[GN * SEG, 128], [SEG, GN], [1, SH]]),
                                    ap3(dub16[dpar], 0, [[SH, 128], [0, GN], [1, SH]]),
                                    ap3(warena, n0 * SH,
                                        [[32832, 128], [SH, GN], [1, SH]]),
                                    ALU.mult)
                            df.step("dve", dubf,
                                    reads=[("du", dpar)] +
                                          [("wa", n0 + j) for j in range(GN)] +
                                          ([("hst", dc, g)] if sh else []),
                                    writes=[("dubg",)])

                            def scan_fn(e, dc=dc, g=g, n0=n0, gpar=gpar, sh=sh):
                                e.tensor_tensor_scan(hbg[gpar][:, :], dabg[gpar][:, :],
                                                     dubg[:, :], 0.0,
                                                     ALU.mult, ALU.add)
                                if sh == 0:
                                    e.tensor_copy(
                                        hstate[:, dc * 16 + n0:dc * 16 + n0 + GN],
                                        ap3(hbg[gpar], SH,
                                            [[GN * SEG, 128], [SEG, GN], [1, 1]]))
                            df.step("dve", scan_fn,
                                    reads=[("dabg", gpar), ("dubg",)],
                                    writes=[("hbg", gpar)] +
                                           ([("hst", dc, g)] if sh == 0 else []))

                            def prodf(e, n0=n0, gpar=gpar):
                                e.tensor_tensor(
                                    ap3(prodg[gpar], 0,
                                        [[GN * SH, 128], [SH, GN], [1, SH]]),
                                    ap3(hbg[gpar], 1,
                                        [[GN * SEG, 128], [SEG, GN], [1, SH]]),
                                    ap3(warena, 16384 + n0 * SH,
                                        [[32832, 128], [SH, GN], [1, SH]]),
                                    ALU.mult)
                            df.step("dve", prodf,
                                    reads=[("hbg", gpar)] +
                                          [("wa", 16 + n0 + j) for j in range(GN)],
                                    writes=[("prodg", gpar)])

                            def mm_y(e, g=g, gpar=gpar):
                                for j in range(GN):
                                    for hh in (0, 512):
                                        e.matmul(psY[:, hh:hh + 512], ident[:, :],
                                                 prodg[gpar][:, j * SH + hh:
                                                             j * SH + hh + 512],
                                                 start=(g == 0 and j == 0),
                                                 stop=(g == NG - 1 and j == GN - 1))
                            df.step("pe", mm_y, mmg=True,
                                    reads=[("prodg", gpar), ("identd",)],
                                    writes=[("psY",)])
                        df.step("dve", lambda e, dc=dc, dpar=dpar:
                                e.scalar_tensor_tensor(y32[:, :], u_sb[dpar][:, :],
                                                       dcol[:, dc:dc + 1], psY[:, :],
                                                       ALU.mult, ALU.add),
                                reads=[("psY",), ("usb", dpar), ("cvecs", dc)],
                                writes=[("y32",)])
                        df.step("dve", lambda e, dpar=dpar:
                                e.tensor_tensor(ybf[dpar][:, :], y32[:, :],
                                                z_sb[dpar][:, :], ALU.mult),
                                reads=[("y32",), ("zsb", dpar)],
                                writes=[("ybf", dpar)])
                        df.dma([(y_hbm[b, dc, :, h0:h0 + SH], ybf[dpar][:, :])],
                               reads=[("ybf", dpar)], writes=[("yh", b, dc, sh)])

                    emit_glue(0)
                    for dc in range(NDC):
                        if dc + 1 < NDC:
                            emit_glue(dc + 1)
                        emit_gloop(dc)
            df.flush(block)
        ab.close()
        df.barrier()

        with ExitStack() as es:
            sbC = lambda name, shape, dt=F32: es.enter_context(
                nc.sbuf_tensor(name, shape, dt))
            ybuf = [sbC("ybufc0", [128, NDC * CS], BF16)]
            xsC = [sbC("xsC0", [128, 8 * CS])] * 2
            xbC = [sbC("xbC0", [128, 8 * CS], BF16)]
            ln1 = [sbC(f"ln1_{i}", [128, 8 * CS]) for i in range(2)]
            identf = sbC("identf", [128, 128])
            onesb = sbC("onesb", [128, 1], BF16)
            onesrow = sbC("onesrow", [1, 128])
            xln1 = sbC("xln1", [128, 8 * CS])
            xlnb = sbC("xlnb", [128, 8 * CS], BF16)
            ffnb = sbC("ffnb", [128, CS])
            nfb = sbC("nfb", [128, 4 * CS], BF16)
            fgb2 = sbC("fgb2", [128, 4 * CS], BF16)
            mrep = [sbC(f"mrep{i}", [128, CS]) for i in range(2)]
            rrep = [sbC(f"rrep{i}", [128, CS]) for i in range(2)]
            bprep = sbC("bprep", [128, CS])
            mrow = sbC("mrow", [1, CS])
            rrow = sbC("rrow", [1, CS])
            erow = sbC("erow", [1, CS])
            vrow = erow
            lgt = sbC("lgt", [128, 32])
            mxc = sbC("mxc", [128, 4])
            smc = sbC("smc", [128, 4])
            bpc = sbC("bpc", [128, CS // 128])
            psN = [es.enter_context(nc.psum_tensor(f"psN{i}", [128, CS], F32))
                   for i in range(2)]
            psF = [es.enter_context(nc.psum_tensor(f"psF{i}", [128, CS], F32))
                   for i in range(2)]
            psS = es.enter_context(nc.psum_tensor("psS", [1, CS], F32))
            psS2 = es.enter_context(nc.psum_tensor("psS2", [1, CS], F32))
            psB = es.enter_context(nc.psum_tensor("psB", [128, CS], F32))
            wstgC = [sbC(f"wstage2_{i}", [128, 1032])
                     for i in range(3)]
            for dc in range(NDC):
                wp = dc % 3
                df.dma([(wstgC[wp][:, 0:1024], w_out[dc * 128:(dc + 1) * 128, :])],
                       writes=[("wstage2", wp)])
                df.step("dve", lambda e, dc=dc, wp=wp: e.tensor_copy(
                    warena[:, dc * 1024:(dc + 1) * 1024], wstgC[wp][:, 0:1024]),
                    reads=[("wstage2", wp)], writes=[("wa", dc)])
            for kc in range(NMC):
                wp = (NDC + kc) % 3
                df.dma([(wstgC[wp][:, 0:1024], w_fg[kc * 128:(kc + 1) * 128, :])],
                       writes=[("wstage2", wp)])
                df.step("dve", lambda e, kc=kc, wp=wp: e.tensor_copy(
                    warena[:, 16384 + kc * 1024:16384 + (kc + 1) * 1024],
                    wstgC[wp][:, 0:1024]),
                    reads=[("wstage2", wp)], writes=[("wa", 16 + kc)])
            for kc in range(NMC):
                wp = (NDC + NMC + kc) % 3
                df.dma([(wstgC[wp][:, 0:1024], w_we[kc * 128:(kc + 1) * 128, :]),
                        (wstgC[wp][:, 1024:1032],
                         w_wg[kc * 128:(kc + 1) * 128, :])],
                       writes=[("wstage2", wp)])

                def cpw(e, kc=kc, wp=wp):
                    e.tensor_copy(warena[:, 24576 + kc * 1024:
                                         24576 + (kc + 1) * 1024],
                                  wstgC[wp][:, 0:1024])
                    e.tensor_copy(w_wg_ap(kc), wstgC[wp][:, 1024:1032])
                df.step("dve", cpw, reads=[("wstage2", wp)],
                        writes=[("wa", 24 + kc), ("wa", 32)],
                        serial=False)
            df.step("dve", lambda e: (e.tensor_copy(identf[:, :], ident[:, :]),
                                      e.memset(onesrow[:, :], 1.0),
                                      e.memset(onesb[:, :], 1.0)),
                    reads=[("identd",)],
                    writes=[("identf",), ("onesrow",), ("onesb",)],
                    serial=False)
            psL = es.enter_context(nc.psum_tensor("psL", [128, 32], F32))

            def ln_stats_chunks(lpar):
                L = ln1[lpar]

                def c0(L=L, lpar=lpar):
                    def stat_mm(e, L=L):
                        for kc in range(NMC):
                            e.matmul(psS[0:1, :], ones32[:, 0:1],
                                     L[:, kc * CS:(kc + 1) * CS],
                                     start=(kc == 0), stop=(kc == NMC - 1))
                    df.step("pe", stat_mm, mmg=True,
                            reads=[("ln1", lpar, kc) for kc in range(NMC)] +
                                  [("ones",)],
                            writes=[("psS",)])
                    df.step("act", lambda e, L=L: e.activation(
                        out=xlnb[:, :], in_=L[:, :], func=AF.Square),
                        reads=[("ln1", lpar, kc) for kc in range(NMC)],
                        writes=[("xlnb", kc) for kc in range(NMC)])

                def c1():
                    def stat_mm2(e):
                        for kc in range(NMC):
                            e.matmul(psS2[0:1, :], onesb[:, 0:1],
                                     xlnb[:, kc * CS:(kc + 1) * CS],
                                     start=(kc == 0), stop=(kc == NMC - 1))
                    df.step("pe", stat_mm2, mmg=True,
                            reads=[("xlnb", kc) for kc in range(NMC)] +
                                  [("onesb",)],
                            writes=[("psS2",)])

                    def rowmath(e):
                        e.tensor_scalar_mul(mrow[:, :], psS[0:1, :], 1.0 / D)
                        e.tensor_scalar_mul(erow[:, :], psS2[0:1, :], 1.0 / D)
                        e.tensor_mul(rrow[:, :], mrow[:, :], mrow[:, :])
                        e.tensor_sub(vrow[:, :], erow[:, :], rrow[:, :])
                        e.tensor_scalar_add(vrow[:, :], vrow[:, :], EPS)
                    df.step("dve", rowmath, reads=[("psS",), ("psS2",)],
                            writes=[("mrow",), ("vrow",), ("rrow",)])
                    df.step("act", lambda e: e.activation(
                        out=rrow[:, :], in_=vrow[:, :], func=AF.Sqrt),
                        reads=[("vrow",)], writes=[("rrow",)])

                def c2(lpar=lpar):
                    df.step("pe", lambda e: e.matmul(
                        psB[:, :], onesrow[:, :], mrow[:, :],
                        start=True, stop=True),
                        reads=[("mrow",), ("onesrow",)], writes=[("psB",)],
                        mmg=True)
                    df.step("dve", lambda e, lpar=lpar: e.tensor_copy(
                        mrep[lpar][:, :], psB[:, :]),
                        reads=[("psB",)], writes=[("mrep", lpar)])
                    df.step("pe", lambda e: e.matmul(
                        psB[:, :], onesrow[:, :], rrow[:, :],
                        start=True, stop=True),
                        reads=[("rrow",), ("onesrow",)], writes=[("psB",)],
                        mmg=True)
                    df.step("dve", lambda e, lpar=lpar: e.reciprocal(
                        rrep[lpar][:, :], psB[:, :]),
                        reads=[("psB",)], writes=[("rrep", lpar)])
                return [c0, c1, c2]

            def front_chunks(b, sc):
                s0 = sc * CS
                par = (b * (S // CS) + sc) % 2
                chunks = []

                def f0(b=b, s0=s0, par=par):
                    ypar = 0
                    yb_base = ybuf[ypar][:, :]
                    yh_base = y_hbm[b, 0, :, s0:s0 + CS]
                    df.dma([(RawAP(yb_base.tensor, yb_base.offset + d0 * CS,
                                   [[NDC * CS, 128], [CS, 4], [1, CS]]),
                             RawAP(yh_base.tensor, yh_base.offset + d0 * 128 * S,
                                   [[S, 128], [128 * S, 4], [1, CS]]))
                            for d0 in range(0, NDC, 4)],
                           reads=[("yh", b, dc, s0 // SH) for dc in range(NDC)],
                           writes=[("ybufc", ypar)])
                    xc_base = xsC[0][:, :]
                    xf_base = x_fm[b, :, s0:s0 + CS]
                    df.dma([(RawAP(xc_base.tensor, xc_base.offset,
                                   [[8 * CS, 128], [CS, 8], [1, CS]]),
                             RawAP(xf_base.tensor, xf_base.offset,
                                   [[S, 128], [128 * S, 8], [1, CS]]))],
                           writes=[("xsc", 0)])
                    df.step("dve", lambda e, par=par: e.tensor_copy(
                        xbC[0][:, :], xsC[0][:, :]),
                        reads=[("xsc", 0)], writes=[("xbc", 0)])
                chunks.append(f0)

                for mc in range(NMC):
                    def fmc(mc=mc, par=par):
                        pp = mc % 2

                        def mmnf(e, mc=mc, pp=pp, par=par):
                            for dc in range(NDC):
                                e.matmul(psN[pp][:, :], w_out_ap(dc, mc),
                                         ybuf[0][:, dc * CS:(dc + 1) * CS],
                                         start=(dc == 0), stop=(dc == NDC - 1))
                            for kc in range(NMC):
                                e.matmul(psF[pp][:, :], w_fg_ap(kc, mc),
                                         xbC[0][:, kc * CS:(kc + 1) * CS],
                                         start=(kc == 0), stop=(kc == NMC - 1))
                        df.step("pe", mmnf, mmg=True,
                                reads=[("ybufc", 0), ("xbc", 0)] +
                                      [("wa", dc) for dc in range(NDC)] +
                                      [("wa", 16 + kc) for kc in range(NMC)],
                                writes=[("psN", pp), ("psF", pp)])
                        mh = mc % 4
                        df.step("act", lambda e, mh=mh, pp=pp, mc=mc:
                                e.activation(
                                    out=fgb2[:, mh * CS:(mh + 1) * CS],
                                    in_=psF[pp][:, :],
                                    func=AF.Sigmoid, bias=fgbc[:, mc:mc + 1]),
                                reads=[("psF", pp), ("cvecs2",)],
                                writes=[("fgb", mh)])
                        df.step("act", lambda e, mh=mh, pp=pp: e.activation(
                            out=nfb[:, mh * CS:(mh + 1) * CS], in_=psN[pp][:, :],
                            func=AF.Copy),
                            reads=[("psN", pp)], writes=[("nf", mh)])
                        if mc % 4 == 3:
                            m0 = mc - 3

                            def gate(e, m0=m0, par=par, lpar=par):
                                sl = slice(m0 * CS, (m0 + 4) * CS)
                                L = ln1[lpar]
                                e.tensor_sub(L[:, sl], xsC[0][:, sl], nfb[:, :])
                                e.tensor_mul(L[:, sl], fgb2[:, :], L[:, sl])
                                e.tensor_add(L[:, sl], L[:, sl], nfb[:, :])
                                e.tensor_add(L[:, sl], L[:, sl], xsC[0][:, sl])
                            df.step("dve", gate,
                                    reads=[("xsc", 0)] +
                                          [("nf", j) for j in range(4)] +
                                          [("fgb", j) for j in range(4)],
                                    writes=[("ln1", par, m0 + j)
                                            for j in range(4)])
                    chunks.append(fmc)
                return chunks

            def tailA_chunks(b, sc):
                par = (b * (S // CS) + sc) % 2
                chunks = list(ln_stats_chunks(par))

                def c_apply1(par=par):
                    def apply1(e, lpar=par):
                        mr = mrep[lpar][:, :]
                        rr = rrep[lpar][:, :]
                        mrv = RawAP(mr.tensor, mr.offset,
                                    [[CS, 128], [0, 8], [1, CS]])
                        rrv = RawAP(rr.tensor, rr.offset,
                                    [[CS, 128], [0, 8], [1, CS]])
                        e.tensor_tensor(xln1[:, :], ln1[lpar][:, :], mrv,
                                        ALU.subtract)
                        e.tensor_tensor(xln1[:, :], xln1[:, :], rrv, ALU.mult)
                        e.tensor_copy(xlnb[:, :], xln1[:, :])
                    df.step("dve", apply1,
                            reads=[("ln1", par, mc) for mc in range(NMC)] +
                                  [("mrep", par), ("rrep", par)],
                            writes=[("xln1", mc) for mc in range(NMC)] +
                                   [("xlnb", mc) for mc in range(NMC)])
                chunks.append(c_apply1)

                def c_mml():
                    def mml(e):
                        for tc in range(CS // 128):
                            for kc in range(NMC):
                                e.matmul(psL[:, tc * 8:(tc + 1) * 8],
                                         xlnb[:, kc * CS + tc * 128:
                                              kc * CS + (tc + 1) * 128],
                                         w_wg_ap(kc), start=(kc == 0),
                                         stop=(kc == NMC - 1))
                    df.step("pe", mml, mmg=True,
                            reads=[("xlnb", kc) for kc in range(NMC)] +
                                  [("wa", 32)],
                            writes=[("psL",)])
                chunks.append(c_mml)

                def c_sm1():
                    def sm1(e):
                        bg4 = bgrep[:, :]
                        e.tensor_tensor(lgt[:, :], psL[:, :],
                                        RawAP(bg4.tensor, bg4.offset,
                                              [[8, 128], [0, 4], [1, 8]]),
                                        ALU.add)
                        for tc in range(CS // 128):
                            e.reduce_max(mxc[:, tc:tc + 1],
                                         lgt[:, tc * 8:(tc + 1) * 8],
                                         axis=mybir.AxisListType.X)
                        for tc in range(CS // 128):
                            e.tensor_scalar(lgt[:, tc * 8:(tc + 1) * 8],
                                            lgt[:, tc * 8:(tc + 1) * 8],
                                            mxc[:, tc:tc + 1], None,
                                            ALU.subtract)
                    df.step("dve", sm1, reads=[("psL",), ("cvecs2",)],
                            writes=[("lgt",), ("mxc",)])
                    df.step("act", lambda e: e.activation(
                        out=lgt[:, :], in_=lgt[:, :], func=AF.Exp),
                        reads=[("lgt",)], writes=[("lgt",)])
                chunks.append(c_sm1)

                def c_sm2():
                    def sm2(e):
                        for tc in range(CS // 128):
                            e.reduce_sum(smc[:, tc:tc + 1],
                                         lgt[:, tc * 8:(tc + 1) * 8],
                                         axis=mybir.AxisListType.X)
                        e.reciprocal(bpc[:, :], smc[:, :])
                    df.step("dve", sm2, reads=[("lgt",)],
                            writes=[("smc",)] + [("bpc", tc) for tc in range(4)])

                    def bp_tr(e):
                        for tc in range(CS // 128):
                            e.matmul(psS[0:1, tc * 128:(tc + 1) * 128],
                                     bpc[:, tc:tc + 1], identf[:, :],
                                     start=True, stop=True)
                    df.step("pe", bp_tr, mmg=True,
                            reads=[("bpc", tc) for tc in range(CS // 128)] +
                                  [("identf",)],
                            writes=[("psS",)])
                    df.step("dve", lambda e: e.tensor_copy(mrow[:, :],
                                                           psS[0:1, :]),
                            reads=[("psS",)], writes=[("mrow",)])
                    df.step("pe", lambda e: e.matmul(
                        psB[:, :], onesrow[:, :], mrow[:, :],
                        start=True, stop=True), mmg=True,
                        reads=[("mrow",), ("onesrow",)], writes=[("psB",)])
                    df.step("dve", lambda e: e.tensor_copy(bprep[:, :],
                                                           psB[:, :]),
                            reads=[("psB",)], writes=[("bprep",)])
                chunks.append(c_sm2)

                for m0 in range(0, NMC, 2):
                    def c_mme(m0=m0, par=par):
                        for mc in range(m0, m0 + 2):
                            pp = mc % 2

                            def mme(e, mc=mc, pp=pp):
                                for kc in range(NMC):
                                    e.matmul(psN[pp][:, :], w_we_ap(kc, mc),
                                             xlnb[:, kc * CS:(kc + 1) * CS],
                                             start=(kc == 0),
                                             stop=(kc == NMC - 1))
                            df.step("pe", mme, mmg=True,
                                    reads=[("xlnb", kc) for kc in range(NMC)] +
                                          [("wa", 24 + kc)
                                           for kc in range(NMC)],
                                    writes=[("psN", pp)])

                            def ffn_res(e, mc=mc, pp=pp, lpar=par):
                                sl = slice(mc * CS, (mc + 1) * CS)
                                e.scalar_tensor_tensor(ffnb[:, :],
                                                       psN[pp][:, :],
                                                       bec[:, mc:mc + 1],
                                                       bprep[:, :],
                                                       ALU.add, ALU.mult)
                                e.tensor_add(ln1[lpar][:, sl], xln1[:, sl],
                                             ffnb[:, :])
                            df.step("dve", ffn_res,
                                    reads=[("psN", pp), ("bprep",),
                                           ("xln1", mc), ("cvecs2",)],
                                    writes=[("ffnb",), ("ln1", par, mc)])
                    chunks.append(c_mme)
                return chunks

            def tailB_chunks(b, sc):
                s0 = sc * CS
                par = (b * (S // CS) + sc) % 2
                chunks = list(ln_stats_chunks(par))

                def c_out(b=b, s0=s0, par=par):
                    def apply2(e, lpar=par):
                        mr = mrep[lpar][:, :]
                        rr = rrep[lpar][:, :]
                        mrv = RawAP(mr.tensor, mr.offset,
                                    [[CS, 128], [0, 8], [1, CS]])
                        rrv = RawAP(rr.tensor, rr.offset,
                                    [[CS, 128], [0, 8], [1, CS]])
                        e.tensor_tensor(xln1[:, :], ln1[lpar][:, :], mrv,
                                        ALU.subtract)
                        e.tensor_tensor(xln1[:, :], xln1[:, :], rrv, ALU.mult)
                    df.step("dve", apply2,
                            reads=[("ln1", par, mc) for mc in range(NMC)] +
                                  [("mrep", par), ("rrep", par)],
                            writes=[("xln1", mc) for mc in range(NMC)])
                    of_base = out_fm[b, :, s0:s0 + CS]
                    xl_base = xln1[:, :]
                    df.dma([(RawAP(of_base.tensor, of_base.offset,
                                   [[S, 128], [128 * S, 8], [1, CS]]),
                             RawAP(xl_base.tensor, xl_base.offset,
                                   [[8 * CS, 128], [CS, 8], [1, CS]]))],
                           reads=[("xln1", mc) for mc in range(NMC)])
                chunks.append(c_out)
                return chunks

            its = [(b, sc) for b in range(BLOC) for sc in range(S // CS)]
            nit = len(its)
            def merge(fc, ta, tb):
                fc = list(fc or [])
                ta = list(ta or [])
                tb = list(tb or [])
                out = fc[:1]
                fseq = fc[1:]
                pre, post = fseq[:3], fseq[3:]
                i = j = 0
                while i < len(pre) or j < len(tb):
                    if i < len(pre):
                        out.append(pre[i]); i += 1
                    if j < len(tb):
                        out.append(tb[j]); j += 1
                i = j = 0
                while i < len(post) or j < len(ta):
                    if i < len(post):
                        out.append(post[i]); i += 1
                    if j < len(ta):
                        out.append(ta[j]); j += 1
                for ch in out:
                    ch()

            for k in range(nit + 2):
                fc = front_chunks(*its[k]) if k < nit else []
                ta = tailA_chunks(*its[k - 1]) if 1 <= k <= nit else []
                tb = tailB_chunks(*its[k - 2]) if 2 <= k <= nit + 1 else []
                merge(fc, ta, tb)
            df.flush(block)
        df.finish(block)
    return nc


_NC_CACHE = []


def kernel(L0_attn_x, L0_attn_in_proj_w, L0_attn_conv_w, L0_attn_conv_b,
           L0_attn_x_proj_w, L0_attn_dt_proj_w, L0_attn_dt_proj_b, L0_attn_A_log,
           L0_attn_D, L0_attn_out_proj_w, L0_attn_fg_w, L0_attn_fg_b,
           L0_ffn_Wg, L0_ffn_bg, L0_ffn_We, L0_ffn_be):
    x = np.asarray(L0_attn_x, dtype=np.float32)
    if not _NC_CACHE:
        _NC_CACHE.append(build_nc())
    nc = _NC_CACHE[0]

    shared = dict(
        w_in=np.ascontiguousarray(L0_attn_in_proj_w, dtype=np.float32),
        w_xp=np.ascontiguousarray(L0_attn_x_proj_w, dtype=np.float32),
        w_dt=np.ascontiguousarray(L0_attn_dt_proj_w, dtype=np.float32),
        w_out=np.ascontiguousarray(L0_attn_out_proj_w, dtype=np.float32),
        w_fg=np.ascontiguousarray(L0_attn_fg_w, dtype=np.float32),
        w_wg=np.ascontiguousarray(L0_ffn_Wg, dtype=np.float32),
        w_we=np.ascontiguousarray(L0_ffn_We, dtype=np.float32),
        conv_w=np.ascontiguousarray(
            np.asarray(L0_attn_conv_w, dtype=np.float32).reshape(DI, 4)),
        conv_b=np.ascontiguousarray(L0_attn_conv_b, dtype=np.float32),
        dt_b=np.ascontiguousarray(L0_attn_dt_proj_b, dtype=np.float32),
        a_log=np.ascontiguousarray(L0_attn_A_log, dtype=np.float32),
        d_vec=np.ascontiguousarray(L0_attn_D, dtype=np.float32),
        fg_b=np.ascontiguousarray(L0_attn_fg_b, dtype=np.float32),
        bg=np.ascontiguousarray(L0_ffn_bg, dtype=np.float32),
        be=np.ascontiguousarray(L0_ffn_be, dtype=np.float32),
    )
    in_maps = []
    for i in range(NCORE):
        m = dict(shared)
        m["x_fm"] = np.ascontiguousarray(
            x[i * BLOC:(i + 1) * BLOC].transpose(0, 2, 1))
        in_maps.append(m)

    res = run_bass_kernel_spmd(nc, in_maps, core_ids=list(range(NCORE)))
    out = np.concatenate(
        [np.asarray(r["out_fm"]).transpose(0, 2, 1) for r in res.results], axis=0)
    return out


# version 37
# speedup vs baseline: 1.0052x; 1.0052x over previous
import numpy as np
from contextlib import ExitStack

import concourse.bass as bass
import concourse.mybir as mybir
from concourse.ap import AP as RawAP
from concourse.bass_utils import run_bass_kernel_spmd

F32 = mybir.dt.float32
BF16 = mybir.dt.bfloat16
AF = mybir.ActivationFunctionType
ALU = mybir.AluOpType

D = 1024
DI = 2048
S = 2048
N = 16
DTR = 64
BLOC = 2
CS = 512
SH = 1024
NCORE = 8
NDC = DI // 128
NMC = D // 128
EPS = 1e-5

_EMITTERS = {
    "tensor_copy", "tensor_add", "tensor_sub", "tensor_mul", "tensor_max",
    "tensor_scalar", "tensor_scalar_add", "tensor_scalar_sub",
    "tensor_scalar_mul", "tensor_scalar_max", "tensor_scalar_min",
    "tensor_single_scalar", "tensor_tensor", "scalar_tensor_tensor",
    "tensor_tensor_scan", "tensor_reduce", "reduce_max", "reduce_sum",
    "reciprocal", "memset", "activation", "matmul", "dma_start",
    "tensor_relu",
}
ENGS = ("sp", "pe", "act", "dve", "pool")
_ENG_ATTR = {"sp": "sync", "pe": "tensor", "act": "scalar",
             "dve": "vector", "pool": "gpsimd"}
NQ = 6


class _CountProxy:
    def __init__(self):
        self.n = 0

    def __getattr__(self, name):
        if name in _EMITTERS:
            def f(*a, **kw):
                self.n += 16 if name == "dma_start" else 1
                return self
            return f
        raise AttributeError(name)

    def then_inc(self, *a):
        return self


class _EmitProxy:
    def __init__(self, eng, sem, start, serial, mmg):
        self._eng = eng
        self._sem = sem
        self._cnt = start
        self._start = start
        self._serial = serial
        self._mmg = mmg

    def __getattr__(self, name):
        if name not in _EMITTERS:
            raise AttributeError(name)
        real = getattr(self._eng, name)

        def f(*a, **kw):
            if (self._serial and self._cnt > self._start
                    and not (name == "matmul" and self._mmg)):
                self._eng.wait_ge(self._sem, self._cnt)
            ins = real(*a, **kw)
            ins.then_inc(self._sem, 16 if name == "dma_start" else 1)
            self._cnt += 16 if name == "dma_start" else 1
            return ins
        return f


class _Step:
    __slots__ = ("eng", "sem_name", "fn", "reads", "writes", "serial",
                 "mmg", "waits", "start", "end", "n")

    def __init__(self, eng, sem_name, fn, reads, writes, serial, mmg):
        self.eng, self.sem_name, self.fn = eng, sem_name, fn
        self.reads, self.writes = list(reads), list(writes)
        self.serial, self.mmg = serial, mmg


class DF:
    def __init__(self, nc, top):
        self.nc = nc
        names = list(ENGS) + [f"q{i}" for i in range(NQ)]
        self.sems = {e: top.enter_context(nc.semaphore("prog_" + e))
                     for e in names}
        self.steps = []
        self._qi = 0
        self.cnt = {s: 0 for s in self.sems}
        self.wm = {e: {s: 0 for s in self.sems} for e in ENGS}
        self.writer = {}
        self.readers = {}
        self.pending = {}

    def barrier(self):
        snap = dict(self.cnt)
        for e in ENGS:
            self.pending[e] = snap

    def step(self, eng, fn, reads=(), writes=(), serial=True, mmg=False):
        self.steps.append(_Step(eng, eng, fn, reads, writes, serial, mmg))

    def dma(self, pairs, reads=(), writes=(), **kw):
        pairs = list(pairs)
        qi = self._qi
        self._qi = (self._qi + 1) % NQ

        def fn(e):
            for dst, src in pairs:
                e.dma_start(out=dst, in_=src, **kw)
        self.steps.append(_Step("sp", f"q{qi}", fn, reads,
                                list(writes) + [("__dmaq__", qi)], False,
                                False))

    def flush(self, block):
        cnt, wm, writer, readers = self.cnt, self.wm, self.writer, self.readers
        for st in self.steps:
            c = _CountProxy()
            st.fn(c)
            st.n = c.n
            waits = {}

            def need(dep, _w=waits, _e=st.eng):
                if dep is None:
                    return
                sname, val = dep
                if val > wm[_e][sname]:
                    _w[sname] = max(_w.get(sname, 0), val)
            if st.eng in self.pending:
                for sname, val in self.pending.pop(st.eng).items():
                    need((sname, val))
            for k in st.reads:
                need(writer.get(k))
            for k in st.writes:
                need(writer.get(k))
                for r in readers.get(k, ()):
                    need(r)
            st.waits = sorted(waits.items())
            for sname, val in st.waits:
                wm[st.eng][sname] = val
            st.start = cnt[st.sem_name]
            cnt[st.sem_name] += st.n
            st.end = cnt[st.sem_name]
            for k in st.writes:
                writer[k] = (st.sem_name, st.end)
                readers[k] = []
            for k in st.reads:
                readers.setdefault(k, []).append((st.sem_name, st.end))

        by_eng = {}
        for st in self.steps:
            by_eng.setdefault(st.eng, []).append(st)
        for eng_name in ENGS:
            sts = by_eng.get(eng_name, [])
            if not sts:
                continue

            def emit(e, sts=sts):
                for st in sts:
                    for sname, val in st.waits:
                        e.wait_ge(self.sems[sname], val)
                    p = _EmitProxy(e, self.sems[st.sem_name], st.start,
                                   st.serial, st.mmg)
                    st.fn(p)
                    assert p._cnt == st.end, (st.eng, st.n, p._cnt - st.start)
            getattr(block, _ENG_ATTR[eng_name])(emit)
        self.steps = []

    def finish(self, block):
        def emit(e):
            for sname, val in self.cnt.items():
                if val > 0:
                    e.wait_ge(self.sems[sname], val)
        block.sync(emit)


def bcast(src_ap, parts, n):
    return RawAP(src_ap.tensor, src_ap.offset, [[0, parts], [1, n]])


def ap3(t, offset, dims):
    base = t[:, :]
    return RawAP(base.tensor, base.offset + offset, dims)


def build_nc():
    nc = bass.Bass("TRN2", target_bir_lowering=False, debug=False)

    x_fm = nc.dram_tensor("x_fm", [BLOC, D, S], F32, kind="ExternalInput").ap()
    w_in = nc.dram_tensor("w_in", [D, 2 * DI], F32, kind="ExternalInput").ap()
    w_xp = nc.dram_tensor("w_xp", [DI, 96], F32, kind="ExternalInput").ap()
    w_dt = nc.dram_tensor("w_dt", [DTR, DI], F32, kind="ExternalInput").ap()
    w_out = nc.dram_tensor("w_out", [DI, D], F32, kind="ExternalInput").ap()
    w_fg = nc.dram_tensor("w_fg", [D, D], F32, kind="ExternalInput").ap()
    w_wg = nc.dram_tensor("w_wg", [D, 8], F32, kind="ExternalInput").ap()
    w_we = nc.dram_tensor("w_we", [D, D], F32, kind="ExternalInput").ap()
    conv_w = nc.dram_tensor("conv_w", [DI, 4], F32, kind="ExternalInput").ap()
    conv_b = nc.dram_tensor("conv_b", [DI], F32, kind="ExternalInput").ap()
    dt_b = nc.dram_tensor("dt_b", [DI], F32, kind="ExternalInput").ap()
    a_log = nc.dram_tensor("a_log", [DI, N], F32, kind="ExternalInput").ap()
    d_vec = nc.dram_tensor("d_vec", [DI], F32, kind="ExternalInput").ap()
    fg_b = nc.dram_tensor("fg_b", [D], F32, kind="ExternalInput").ap()
    bg = nc.dram_tensor("bg", [8], F32, kind="ExternalInput").ap()
    be = nc.dram_tensor("be", [D], F32, kind="ExternalInput").ap()
    out_fm = nc.dram_tensor("out_fm", [BLOC, D, S], F32,
                            kind="ExternalOutput").ap()

    u_hbm = nc.dram_tensor("u_hbm", [BLOC, NDC, 128, S], BF16, kind="Internal").ap()
    z_hbm = nc.dram_tensor("z_hbm", [BLOC, NDC, 128, S], BF16, kind="Internal").ap()
    y_hbm = nc.dram_tensor("y_hbm", [BLOC, NDC, 128, S], BF16, kind="Internal").ap()
    xdblb_hbm = nc.dram_tensor("xdblb_hbm", [32, BLOC * S], BF16, kind="Internal").ap()
    rows_hbm = nc.dram_tensor("rows_hbm", [4, CS], F32, kind="Internal").ap()
    ident_hbm = nc.dram_tensor("ident_hbm", [128, 128], BF16, kind="Internal").ap()

    with ExitStack() as top:
        block = top.enter_context(nc.Block())
        df = DF(nc, top)
        sb = lambda name, shape, dt=F32: top.enter_context(
            nc.sbuf_tensor(name, shape, dt))

        warena = sb("warena", [128, 32832], BF16)
        fgbc = sb("fgbc", [128, NMC], F32)
        bec = sb("bec", [128, NMC], F32)
        bgrep = sb("bgrep", [128, 8], F32)
        ones32 = sb("ones32", [128, 1], F32)
        ident = sb("ident", [128, 128], BF16)
        ab = top.enter_context(ExitStack())
        sbab = lambda name, shape, dt=F32: ab.enter_context(
            nc.sbuf_tensor(name, shape, dt))
        wxp = sbab("wxp", [128, NDC * 96], BF16)
        wdt = sbab("wdt", [64, DI], BF16)
        xdbl_b = sbab("xdbl_b", [96, BLOC * S], BF16)
        acol = sbab("acol", [128, NDC * N], F32)
        cw = sbab("cw", [128, NDC * 4], F32)
        cb = sbab("cb", [128, NDC], F32)
        dtbc = sbab("dtbc", [128, NDC], F32)
        dcol = sbab("dcol", [128, NDC], F32)
        tails = sbab("tails", [128, NDC * 3], F32)
        hstate = sbab("hstate", [128, NDC * N], F32)

        w_in_ap = lambda kc, mc: warena[:, kc * 4096 + mc * 128:
                                        kc * 4096 + (mc + 1) * 128]
        w_out_ap = lambda dc, mc: warena[:, dc * 1024 + mc * 128:
                                         dc * 1024 + (mc + 1) * 128]
        w_fg_ap = lambda kc, mc: warena[:, 16384 + kc * 1024 + mc * 128:
                                        16384 + kc * 1024 + (mc + 1) * 128]
        w_we_ap = lambda kc, mc: warena[:, 24576 + kc * 1024 + mc * 128:
                                        24576 + kc * 1024 + (mc + 1) * 128]
        w_wg_ap = lambda kc: warena[:, 32768 + kc * 8: 32768 + (kc + 1) * 8]
        brep_ap = lambda n: warena[:, n * SH:(n + 1) * SH]
        crep_ap = lambda n: warena[:, 16384 + n * SH: 16384 + (n + 1) * SH]

        with ExitStack() as es:
            wstg = [es.enter_context(nc.sbuf_tensor(f"wstage{i}", [128, 2048],
                                                    F32)) for i in range(2)]
            wstage = wstg[0]
            vstg = [es.enter_context(nc.sbuf_tensor(f"vstage{i}", [128, 96],
                                                    F32)) for i in range(2)]
            vstage = vstg[0]

            onebf = es.enter_context(nc.sbuf_tensor("onebf", [128, 1], BF16))

            def prep_const(e):
                e.memset(ones32[:, :], 1.0)
                e.memset(ident[:, :], 0.0)
                e.memset(onebf[:, :], 1.0)
            df.step("dve", prep_const,
                    writes=[("ones",), ("ident",), ("onebf",)], serial=False)
            df.dma([(ident_hbm, ident[:, :])],
                   reads=[("ident",)], writes=[("idh",)])
            diag = RawAP(ident_hbm.tensor, ident_hbm.offset, [[129, 128], [1, 1]])
            df.dma([(diag, onebf[:, :])], reads=[("onebf",)], writes=[("idh",)],
                   allow_slow_non_contiguous=True)
            df.dma([(ident[:, :], ident_hbm)],
                   reads=[("idh",)], writes=[("identd",)])

            for hr in range(16):
                kc, h = divmod(hr, 2)
                wp = hr % 2
                df.dma([(wstg[wp][:, :],
                         w_in[kc * 128:(kc + 1) * 128,
                              h * 2048:(h + 1) * 2048])],
                       writes=[("wstage", wp)])
                df.step("dve", lambda e, kc=kc, h=h, wp=wp: e.tensor_copy(
                    warena[:, kc * 4096 + h * 2048:kc * 4096 + (h + 1) * 2048],
                    wstg[wp][:, :]),
                    reads=[("wstage", wp)],
                    writes=[("wa", 4 * kc + 2 * h + j) for j in range(2)])
            for dc in range(NDC):
                vp = dc % 2
                df.dma([(vstg[vp][:, 0:96], w_xp[dc * 128:(dc + 1) * 128, :])],
                       writes=[("vstage", vp)])
                df.step("dve", lambda e, dc=dc, vp=vp: e.tensor_copy(
                    wxp[:, dc * 96:(dc + 1) * 96], vstg[vp][:, 0:96]),
                    reads=[("vstage", vp)], writes=[("wxp", dc)])
            df.dma([(wstage[0:64, 0:DI], w_dt)], writes=[("wstage", 0)])
            df.step("dve", lambda e: e.tensor_copy(wdt[:, :], wstage[0:64, 0:DI]),
                    reads=[("wstage", 0)], writes=[("wdt",)])
            for dc in range(NDC):
                df.dma([(vstage[:, 0:16], a_log[dc * 128:(dc + 1) * 128, :]),
                        (cw[:, dc * 4:(dc + 1) * 4], conv_w[dc * 128:(dc + 1) * 128, :]),
                        (cb[:, dc:dc + 1], conv_b[dc * 128:(dc + 1) * 128].unsqueeze(1)),
                        (dtbc[:, dc:dc + 1], dt_b[dc * 128:(dc + 1) * 128].unsqueeze(1)),
                        (dcol[:, dc:dc + 1], d_vec[dc * 128:(dc + 1) * 128].unsqueeze(1))],
                       writes=[("vstage", 0), ("cvecs", dc)])
                df.step("act", lambda e, dc=dc: e.activation(
                    out=acol[:, dc * 16:(dc + 1) * 16], in_=vstage[:, 0:16],
                    func=AF.Exp), reads=[("vstage", 0)], writes=[("acole", dc)])
                df.step("dve", lambda e, dc=dc: e.tensor_scalar_mul(
                    acol[:, dc * 16:(dc + 1) * 16],
                    acol[:, dc * 16:(dc + 1) * 16], -1.0),
                    reads=[("acole", dc)], writes=[("acol", dc)])
            dmas = []
            for mc in range(NMC):
                dmas.append((fgbc[:, mc:mc + 1], fg_b[mc * 128:(mc + 1) * 128].unsqueeze(1)))
                dmas.append((bec[:, mc:mc + 1], be[mc * 128:(mc + 1) * 128].unsqueeze(1)))
            dmas.append((bgrep[:, :], bcast(bg.unsqueeze(0), 128, 8)))
            df.dma(dmas, writes=[("cvecs2",)])

            xs = [es.enter_context(nc.sbuf_tensor(f"xs{i}", [128, 8 * CS], F32))
                  for i in range(2)]
            xbb = [es.enter_context(nc.sbuf_tensor(f"xb{i}", [128, 8 * CS], BF16))
                   for i in range(2)]
            uball = [es.enter_context(nc.sbuf_tensor("ub0", [128, NDC * CS], BF16))]
            z_all = es.enter_context(nc.sbuf_tensor("z_all", [128, NDC * CS], BF16))
            xi_pad = [es.enter_context(nc.sbuf_tensor(f"xi_pad{i}", [128, CS + 3], BF16))
                      for i in range(2)]
            cacc = [es.enter_context(nc.sbuf_tensor(f"cacc{i}", [128, CS], BF16))
                    for i in range(2)]
            tailsb = es.enter_context(nc.sbuf_tensor("tailsb", [128, NDC * 3], BF16))
            xdbl = es.enter_context(nc.sbuf_tensor("xdbl", [96, BLOC * S], F32))
            psA = [es.enter_context(nc.psum_tensor(f"psA{i}", [128, CS], F32))
                   for i in range(4)]
            psX = es.enter_context(nc.psum_tensor("psX", [96, CS], F32))

            def load_xs(it):
                b, sc = divmod(it, S // CS)
                par = it % 2
                xsb = xs[par][:, :]
                xfb = x_fm[b, :, sc * CS:sc * CS + CS]
                df.dma([(RawAP(xsb.tensor, xsb.offset,
                               [[8 * CS, 128], [CS, 8], [1, CS]]),
                         RawAP(xfb.tensor, xfb.offset,
                               [[S, 128], [128 * S, 8], [1, CS]]))],
                       writes=[("xs", par)])
            load_xs(0)
            for b in range(BLOC):
                df.step("dve", lambda e: e.memset(tailsb[:, :], 0.0),
                        writes=[("tails", dc) for dc in range(NDC)],
                        serial=False)
                for sc in range(S // CS):
                    s0 = sc * CS
                    it = b * (S // CS) + sc
                    par = it % 2
                    df.step("dve", lambda e, par=par: e.tensor_copy(
                        xbb[par][:, :], xs[par][:, :]),
                        reads=[("xs", par)], writes=[("xb", par)])
                    if it + 1 < BLOC * (S // CS):
                        load_xs(it + 1)
                    def emit_xicopy(dc):
                        pp = dc % 4
                        xp = dc % 2
                        df.step("act", lambda e, pp=pp, xp=xp: e.activation(
                            out=xi_pad[xp][:, 3:3 + CS], in_=psA[pp][:, :],
                            func=AF.Copy), reads=[("psA", pp)],
                            writes=[("xim", xp)])

                    def emit_conv(dc):
                        xp = dc % 2

                        def conv_fn(e, dc=dc, xp=xp):
                            xip = xi_pad[xp]
                            cc = cacc[xp]
                            e.tensor_copy(xip[:, 0:3],
                                          tailsb[:, dc * 3:(dc + 1) * 3])
                            e.tensor_copy(tailsb[:, dc * 3:(dc + 1) * 3],
                                          xip[:, CS:CS + 3])
                            e.tensor_scalar(cc[:, :], xip[:, 0:CS],
                                            cw[:, dc * 4:dc * 4 + 1], None,
                                            ALU.mult)
                            for k in (1, 2, 3):
                                e.scalar_tensor_tensor(
                                    cc[:, :], xip[:, k:k + CS],
                                    cw[:, dc * 4 + k:dc * 4 + k + 1],
                                    cc[:, :], ALU.mult, ALU.add)
                        df.step("dve", conv_fn,
                                reads=[("xim", xp), ("tails", dc), ("cvecs", dc)],
                                writes=[("cacc", xp), ("tails", dc)])
                        df.step("act", lambda e, dc=dc, xp=xp: e.activation(
                            out=uball[0][:, dc * CS:(dc + 1) * CS],
                            in_=cacc[xp][:, :], func=AF.Silu,
                            bias=cb[:, dc:dc + 1]),
                            reads=[("cacc", xp), ("cvecs", dc)],
                            writes=[("ub", 0, dc)])

                    for mc in range(32):
                        pp = mc % 4

                        def mm_in(e, mc=mc, pp=pp, par=par):
                            for kc in range(8):
                                e.matmul(psA[pp][:, :], w_in_ap(kc, mc),
                                         xbb[par][:, kc * CS:(kc + 1) * CS],
                                         start=(kc == 0), stop=(kc == 7))
                        df.step("pe", mm_in, mmg=True,
                                reads=[("xb", par)] + [("wa", 4 * kc + mc // 8) for kc in range(8)],
                                writes=[("psA", pp)])
                        if mc < 16:
                            emit_xicopy(mc)
                            if mc >= 1:
                                emit_conv(mc - 1)
                            if mc == 15:
                                emit_conv(15)
                        else:
                            zc = mc - 16
                            df.step("act", lambda e, pp=pp, zc=zc: e.activation(
                                out=z_all[:, zc * CS:(zc + 1) * CS],
                                in_=psA[pp][:, :], func=AF.Silu),
                                reads=[("psA", pp)], writes=[("za", zc)])


                    uab = uball[0][:, :]
                    uhb = u_hbm[b, 0, :, s0:s0 + CS]
                    df.dma([(RawAP(uhb.tensor, uhb.offset + d0 * 128 * S,
                                   [[S, 128], [128 * S, 4], [1, CS]]),
                             RawAP(uab.tensor, uab.offset + d0 * CS,
                                   [[NDC * CS, 128], [CS, 4], [1, CS]]))
                            for d0 in range(0, NDC, 4)],
                           reads=[("ub", 0, dc) for dc in range(NDC)],
                           writes=[("uh", b, dc, sc) for dc in range(NDC)])

                    def mm_xp(e, par=par):
                        for dc in range(NDC):
                            e.matmul(psX[:, :], wxp[:, dc * 96:(dc + 1) * 96],
                                     uball[0][:, dc * CS:(dc + 1) * CS],
                                     start=(dc == 0), stop=(dc == NDC - 1))
                    df.step("pe", mm_xp, mmg=True,
                            reads=[("ub", 0, dc) for dc in range(NDC)] +
                                  [("wxp", dc) for dc in range(NDC)],
                            writes=[("psX",)])
                    zab = z_all[:, :]
                    zhb = z_hbm[b, 0, :, s0:s0 + CS]
                    df.dma([(RawAP(zhb.tensor, zhb.offset + d0 * 128 * S,
                                   [[S, 128], [128 * S, 4], [1, CS]]),
                             RawAP(zab.tensor, zab.offset + d0 * CS,
                                   [[NDC * CS, 128], [CS, 4], [1, CS]]))
                            for d0 in range(0, NDC, 4)],
                           reads=[("za", zc) for zc in range(NDC)],
                           writes=[("zh", b, zc, sc) for zc in range(NDC)])
                    df.step("dve", lambda e, b=b, s0=s0: e.tensor_copy(
                        xdbl[:, b * S + s0: b * S + s0 + CS], psX[:, :]),
                        reads=[("psX",)], writes=[("xdbl", b, s0)])
            df.step("dve", lambda e: e.tensor_copy(xdbl_b[:, :], xdbl[:, :]),
                    reads=[("xdbl", b, sc * CS) for b in range(BLOC)
                           for sc in range(S // CS)],
                    writes=[("xdblb",)])
            df.dma([(xdblb_hbm[:, :], xdbl_b[64:96, :])],
                   reads=[("xdblb",)], writes=[("xdblh",)])
            df.flush(block)
        df.barrier()

        with ExitStack() as es:
            sbB = lambda name, shape, dt=F32: es.enter_context(
                nc.sbuf_tensor(name, shape, dt))
            GN = 4
            NG = N // GN
            SEG = SH + 1
            dabg = [sbB(f"dabg{i}", [128, GN * SEG]) for i in range(2)]
            dubg = sbB("dubg", [128, GN * SEG], BF16)
            hbg = [sbB(f"hbg{i}", [128, GN * SEG], BF16) for i in range(2)]
            prodg = [sbB(f"prodg{i}", [128, GN * SH], BF16) for i in range(2)]
            dt32 = [sbB(f"dt32{i}", [128, SH]) for i in range(2)]
            rawb = sbB("rawb", [128, SH])
            tmpb = sbB("tmpb", [128, SH])
            dub16 = [sbB(f"dub16{i}", [128, SH], BF16) for i in range(2)]
            u_sb = [sbB(f"usb{i}", [128, SH], BF16) for i in range(2)]
            z_sb = [sbB(f"zsb{i}", [128, SH], BF16) for i in range(2)]
            y32 = sbB("y32", [128, SH])
            ybf = [sbB(f"ybf{i}", [128, SH], BF16) for i in range(2)]
            psD = es.enter_context(nc.psum_tensor("psD", [128, SH], F32))
            psY = es.enter_context(nc.psum_tensor("psY", [128, SH], F32))

            def zero_dabg(e):
                e.memset(dabg[0][:, :], 0.0)
                e.memset(dabg[1][:, :], 0.0)
            df.step("dve", zero_dabg, writes=[("dabg", 0), ("dabg", 1)],
                    serial=False)

            def seg_funcs(b, sh):
                    h0 = sh * SH
                    col0 = b * S + h0

                    def emit_dmas():
                        wab = warena[:, :]
                        for grp in range(4):
                            r0 = grp * 8
                            xh = xdblb_hbm[r0, col0:col0 + SH]
                            df.dma([(RawAP(wab.tensor, wab.offset + r0 * SH,
                                           [[32832, 128], [SH, 8], [1, SH]]),
                                     RawAP(xh.tensor, xh.offset,
                                           [[0, 128], [BLOC * S, 8], [1, SH]]))],
                                   reads=[("xdblh",)],
                                   writes=[("wa", r0 + j) for j in range(8)])

                    def emit_glue(dc):
                        dpar = dc % 2

                        def mm_dt(e, dc=dc, col0=col0):
                            for hh in (0, 512):
                                e.matmul(psD[:, hh:hh + 512],
                                         wdt[:, dc * 128:(dc + 1) * 128],
                                         xdbl_b[0:64, col0 + hh:col0 + hh + 512],
                                         start=True, stop=True)
                        df.step("pe", mm_dt, mmg=True,
                                reads=[("xdblb",), ("wdt",)], writes=[("psD",)])

                        df.step("act", lambda e, dc=dc: e.activation(
                            out=rawb[:, :], in_=psD[:, :], func=AF.Identity,
                            bias=dtbc[:, dc:dc + 1]),
                            reads=[("psD",), ("cvecs", dc)], writes=[("rawb",)])
                        df.step("dve", lambda e: e.tensor_scalar(
                            tmpb[:, :], rawb[:, :], 30.0, None, ALU.min),
                            reads=[("rawb",)], writes=[("tmpb",)])

                        def sp2(e):
                            e.activation(out=tmpb[:, :], in_=tmpb[:, :], func=AF.Exp)
                            e.activation(out=tmpb[:, :], in_=tmpb[:, :], func=AF.Ln,
                                         bias=1.0)
                        df.step("act", sp2, reads=[("tmpb",)], writes=[("tmpb",)])
                        df.step("dve", lambda e, dpar=dpar: e.tensor_max(
                            dt32[dpar][:, :], tmpb[:, :], rawb[:, :]),
                            reads=[("tmpb",), ("rawb",)], writes=[("dt", dpar)])
                        df.dma([(u_sb[dpar][:, :], u_hbm[b, dc, :, h0:h0 + SH]),
                                (z_sb[dpar][:, :], z_hbm[b, dc, :, h0:h0 + SH])],
                               reads=[("uh", b, dc, sh * 2), ("uh", b, dc, sh * 2 + 1),
                                      ("zh", b, dc, sh * 2), ("zh", b, dc, sh * 2 + 1)],
                               writes=[("usb", dpar), ("zsb", dpar)])
                        df.step("dve", lambda e, dpar=dpar: e.tensor_tensor(
                            dub16[dpar][:, :], dt32[dpar][:, :], u_sb[dpar][:, :],
                            ALU.mult),
                            reads=[("dt", dpar), ("usb", dpar)], writes=[("du", dpar)])


                    def emit_gloop(dc):
                        dpar = dc % 2
                        for g in range(NG):
                            gpar = g % 2
                            n0 = g * GN

                            def dabf(e, dc=dc, n0=n0, gpar=gpar, dpar=dpar):
                                for j in range(GN):
                                    e.activation(
                                        out=dabg[gpar][:, j * SEG + 1:
                                                       j * SEG + 1 + SH],
                                        in_=dt32[dpar][:, :], func=AF.Exp,
                                        scale=acol[:, dc * 16 + n0 + j:
                                                   dc * 16 + n0 + j + 1])
                            df.step("act", dabf, serial=False,
                                    reads=[("dt", dpar), ("acol", dc)],
                                    writes=[("dabg", gpar)])

                            def dubf(e, dc=dc, g=g, n0=n0, sh=sh, dpar=dpar):
                                icols = ap3(dubg, 0,
                                            [[GN * SEG, 128], [SEG, GN], [1, 1]])
                                if sh == 0:
                                    e.memset(icols, 0.0)
                                else:
                                    e.tensor_copy(
                                        icols, hstate[:, dc * 16 + n0:
                                                      dc * 16 + n0 + GN])
                                e.tensor_tensor(
                                    ap3(dubg, 1, [[GN * SEG, 128], [SEG, GN], [1, SH]]),
                                    ap3(dub16[dpar], 0, [[SH, 128], [0, GN], [1, SH]]),
                                    ap3(warena, n0 * SH,
                                        [[32832, 128], [SH, GN], [1, SH]]),
                                    ALU.mult)
                            df.step("dve", dubf,
                                    reads=[("du", dpar)] +
                                          [("wa", n0 + j) for j in range(GN)] +
                                          ([("hst", dc, g)] if sh else []),
                                    writes=[("dubg",)])

                            def scan_fn(e, dc=dc, g=g, n0=n0, gpar=gpar, sh=sh):
                                e.tensor_tensor_scan(hbg[gpar][:, :], dabg[gpar][:, :],
                                                     dubg[:, :], 0.0,
                                                     ALU.mult, ALU.add)
                                if sh == 0:
                                    e.tensor_copy(
                                        hstate[:, dc * 16 + n0:dc * 16 + n0 + GN],
                                        ap3(hbg[gpar], SH,
                                            [[GN * SEG, 128], [SEG, GN], [1, 1]]))
                            df.step("dve", scan_fn,
                                    reads=[("dabg", gpar), ("dubg",)],
                                    writes=[("hbg", gpar)] +
                                           ([("hst", dc, g)] if sh == 0 else []))

                            def prodf(e, n0=n0, gpar=gpar):
                                e.tensor_tensor(
                                    ap3(prodg[gpar], 0,
                                        [[GN * SH, 128], [SH, GN], [1, SH]]),
                                    ap3(hbg[gpar], 1,
                                        [[GN * SEG, 128], [SEG, GN], [1, SH]]),
                                    ap3(warena, 16384 + n0 * SH,
                                        [[32832, 128], [SH, GN], [1, SH]]),
                                    ALU.mult)
                            df.step("dve", prodf,
                                    reads=[("hbg", gpar)] +
                                          [("wa", 16 + n0 + j) for j in range(GN)],
                                    writes=[("prodg", gpar)])

                            def mm_y(e, g=g, gpar=gpar):
                                for j in range(GN):
                                    for hh in (0, 512):
                                        e.matmul(psY[:, hh:hh + 512], ident[:, :],
                                                 prodg[gpar][:, j * SH + hh:
                                                             j * SH + hh + 512],
                                                 start=(g == 0 and j == 0),
                                                 stop=(g == NG - 1 and j == GN - 1))
                            df.step("pe", mm_y, mmg=True,
                                    reads=[("prodg", gpar), ("identd",)],
                                    writes=[("psY",)])
                        df.step("dve", lambda e, dc=dc, dpar=dpar:
                                e.scalar_tensor_tensor(y32[:, :], u_sb[dpar][:, :],
                                                       dcol[:, dc:dc + 1], psY[:, :],
                                                       ALU.mult, ALU.add),
                                reads=[("psY",), ("usb", dpar), ("cvecs", dc)],
                                writes=[("y32",)])
                        df.step("dve", lambda e, dpar=dpar:
                                e.tensor_tensor(ybf[dpar][:, :], y32[:, :],
                                                z_sb[dpar][:, :], ALU.mult),
                                reads=[("y32",), ("zsb", dpar)],
                                writes=[("ybf", dpar)])
                        df.dma([(y_hbm[b, dc, :, h0:h0 + SH], ybf[dpar][:, :])],
                               reads=[("ybf", dpar)], writes=[("yh", b, dc, sh)])

                    return emit_dmas, emit_glue, emit_gloop

            segsB = [seg_funcs(b, sh)
                     for b in range(BLOC) for sh in range(S // SH)]
            segsB[0][0]()
            segsB[0][1](0)
            for si, (dmas_f, glue_f, gloop_f) in enumerate(segsB):
                for dc in range(NDC):
                    if dc + 1 < NDC:
                        glue_f(dc + 1)
                    elif si + 1 < len(segsB):
                        segsB[si + 1][1](0)
                    gloop_f(dc)
                if si + 1 < len(segsB):
                    segsB[si + 1][0]()
            df.flush(block)
        ab.close()
        df.barrier()

        with ExitStack() as es:
            sbC = lambda name, shape, dt=F32: es.enter_context(
                nc.sbuf_tensor(name, shape, dt))
            ybuf = [sbC("ybufc0", [128, NDC * CS], BF16)]
            xsC = [sbC("xsC0", [128, 8 * CS])] * 2
            xbC = [sbC("xbC0", [128, 8 * CS], BF16)]
            ln1 = [sbC(f"ln1_{i}", [128, 8 * CS]) for i in range(2)]
            identf = sbC("identf", [128, 128])
            onesb = sbC("onesb", [128, 1], BF16)
            onesrow = sbC("onesrow", [1, 128])
            xln1 = sbC("xln1", [128, 8 * CS])
            xlnb = sbC("xlnb", [128, 8 * CS], BF16)
            ffnb = sbC("ffnb", [128, CS])
            nfb = sbC("nfb", [128, 4 * CS], BF16)
            fgb2 = sbC("fgb2", [128, 4 * CS], BF16)
            mrep = [sbC(f"mrep{i}", [128, CS]) for i in range(2)]
            rrep = [sbC(f"rrep{i}", [128, CS]) for i in range(2)]
            bprep = sbC("bprep", [128, CS])
            mrow = sbC("mrow", [1, CS])
            rrow = sbC("rrow", [1, CS])
            erow = sbC("erow", [1, CS])
            vrow = erow
            lgt = sbC("lgt", [128, 32])
            mxc = sbC("mxc", [128, 4])
            smc = sbC("smc", [128, 4])
            bpc = sbC("bpc", [128, CS // 128])
            psN = [es.enter_context(nc.psum_tensor(f"psN{i}", [128, CS], F32))
                   for i in range(2)]
            psF = [es.enter_context(nc.psum_tensor(f"psF{i}", [128, CS], F32))
                   for i in range(2)]
            psS = es.enter_context(nc.psum_tensor("psS", [1, CS], F32))
            psS2 = es.enter_context(nc.psum_tensor("psS2", [1, CS], F32))
            psB = es.enter_context(nc.psum_tensor("psB", [128, CS], F32))
            wstgC = [sbC(f"wstage2_{i}", [128, 1032])
                     for i in range(3)]
            for dc in range(NDC):
                wp = dc % 3
                df.dma([(wstgC[wp][:, 0:1024], w_out[dc * 128:(dc + 1) * 128, :])],
                       writes=[("wstage2", wp)])
                df.step("dve", lambda e, dc=dc, wp=wp: e.tensor_copy(
                    warena[:, dc * 1024:(dc + 1) * 1024], wstgC[wp][:, 0:1024]),
                    reads=[("wstage2", wp)], writes=[("wa", dc)])
            for kc in range(NMC):
                wp = (NDC + kc) % 3
                df.dma([(wstgC[wp][:, 0:1024], w_fg[kc * 128:(kc + 1) * 128, :])],
                       writes=[("wstage2", wp)])
                df.step("dve", lambda e, kc=kc, wp=wp: e.tensor_copy(
                    warena[:, 16384 + kc * 1024:16384 + (kc + 1) * 1024],
                    wstgC[wp][:, 0:1024]),
                    reads=[("wstage2", wp)], writes=[("wa", 16 + kc)])
            for kc in range(NMC):
                wp = (NDC + NMC + kc) % 3
                df.dma([(wstgC[wp][:, 0:1024], w_we[kc * 128:(kc + 1) * 128, :]),
                        (wstgC[wp][:, 1024:1032],
                         w_wg[kc * 128:(kc + 1) * 128, :])],
                       writes=[("wstage2", wp)])

                def cpw(e, kc=kc, wp=wp):
                    e.tensor_copy(warena[:, 24576 + kc * 1024:
                                         24576 + (kc + 1) * 1024],
                                  wstgC[wp][:, 0:1024])
                    e.tensor_copy(w_wg_ap(kc), wstgC[wp][:, 1024:1032])
                df.step("dve", cpw, reads=[("wstage2", wp)],
                        writes=[("wa", 24 + kc), ("wa", 32)],
                        serial=False)
            df.step("dve", lambda e: (e.tensor_copy(identf[:, :], ident[:, :]),
                                      e.memset(onesrow[:, :], 1.0),
                                      e.memset(onesb[:, :], 1.0)),
                    reads=[("identd",)],
                    writes=[("identf",), ("onesrow",), ("onesb",)],
                    serial=False)
            psL = es.enter_context(nc.psum_tensor("psL", [128, 32], F32))

            def ln_stats_chunks(lpar):
                L = ln1[lpar]

                def c0(L=L, lpar=lpar):
                    def stat_mm(e, L=L):
                        for kc in range(NMC):
                            e.matmul(psS[0:1, :], ones32[:, 0:1],
                                     L[:, kc * CS:(kc + 1) * CS],
                                     start=(kc == 0), stop=(kc == NMC - 1))
                    df.step("pe", stat_mm, mmg=True,
                            reads=[("ln1", lpar, kc) for kc in range(NMC)] +
                                  [("ones",)],
                            writes=[("psS",)])
                    df.step("act", lambda e, L=L: e.activation(
                        out=xlnb[:, :], in_=L[:, :], func=AF.Square),
                        reads=[("ln1", lpar, kc) for kc in range(NMC)],
                        writes=[("xlnb", kc) for kc in range(NMC)])

                def c1():
                    def stat_mm2(e):
                        for kc in range(NMC):
                            e.matmul(psS2[0:1, :], onesb[:, 0:1],
                                     xlnb[:, kc * CS:(kc + 1) * CS],
                                     start=(kc == 0), stop=(kc == NMC - 1))
                    df.step("pe", stat_mm2, mmg=True,
                            reads=[("xlnb", kc) for kc in range(NMC)] +
                                  [("onesb",)],
                            writes=[("psS2",)])

                    def rowmath(e):
                        e.tensor_scalar_mul(mrow[:, :], psS[0:1, :], 1.0 / D)
                        e.tensor_scalar_mul(erow[:, :], psS2[0:1, :], 1.0 / D)
                        e.tensor_mul(rrow[:, :], mrow[:, :], mrow[:, :])
                        e.tensor_sub(vrow[:, :], erow[:, :], rrow[:, :])
                        e.tensor_scalar_add(vrow[:, :], vrow[:, :], EPS)
                    df.step("dve", rowmath, reads=[("psS",), ("psS2",)],
                            writes=[("mrow",), ("vrow",), ("rrow",)])
                    df.step("act", lambda e: e.activation(
                        out=rrow[:, :], in_=vrow[:, :], func=AF.Sqrt),
                        reads=[("vrow",)], writes=[("rrow",)])

                def c2(lpar=lpar):
                    df.step("pe", lambda e: e.matmul(
                        psB[:, :], onesrow[:, :], mrow[:, :],
                        start=True, stop=True),
                        reads=[("mrow",), ("onesrow",)], writes=[("psB",)],
                        mmg=True)
                    df.step("dve", lambda e, lpar=lpar: e.tensor_copy(
                        mrep[lpar][:, :], psB[:, :]),
                        reads=[("psB",)], writes=[("mrep", lpar)])
                    df.step("pe", lambda e: e.matmul(
                        psB[:, :], onesrow[:, :], rrow[:, :],
                        start=True, stop=True),
                        reads=[("rrow",), ("onesrow",)], writes=[("psB",)],
                        mmg=True)
                    df.step("dve", lambda e, lpar=lpar: e.reciprocal(
                        rrep[lpar][:, :], psB[:, :]),
                        reads=[("psB",)], writes=[("rrep", lpar)])
                return [c0, c1, c2]

            def front_chunks(b, sc):
                s0 = sc * CS
                par = (b * (S // CS) + sc) % 2
                chunks = []

                def f0(b=b, s0=s0, par=par):
                    ypar = 0
                    yb_base = ybuf[ypar][:, :]
                    yh_base = y_hbm[b, 0, :, s0:s0 + CS]
                    df.dma([(RawAP(yb_base.tensor, yb_base.offset + d0 * CS,
                                   [[NDC * CS, 128], [CS, 4], [1, CS]]),
                             RawAP(yh_base.tensor, yh_base.offset + d0 * 128 * S,
                                   [[S, 128], [128 * S, 4], [1, CS]]))
                            for d0 in range(0, NDC, 4)],
                           reads=[("yh", b, dc, s0 // SH) for dc in range(NDC)],
                           writes=[("ybufc", ypar)])
                    xc_base = xsC[0][:, :]
                    xf_base = x_fm[b, :, s0:s0 + CS]
                    df.dma([(RawAP(xc_base.tensor, xc_base.offset,
                                   [[8 * CS, 128], [CS, 8], [1, CS]]),
                             RawAP(xf_base.tensor, xf_base.offset,
                                   [[S, 128], [128 * S, 8], [1, CS]]))],
                           writes=[("xsc", 0)])
                    df.step("dve", lambda e, par=par: e.tensor_copy(
                        xbC[0][:, :], xsC[0][:, :]),
                        reads=[("xsc", 0)], writes=[("xbc", 0)])
                chunks.append(f0)

                for mc in range(NMC):
                    def fmc(mc=mc, par=par):
                        pp = mc % 2

                        def mmnf(e, mc=mc, pp=pp, par=par):
                            for dc in range(NDC):
                                e.matmul(psN[pp][:, :], w_out_ap(dc, mc),
                                         ybuf[0][:, dc * CS:(dc + 1) * CS],
                                         start=(dc == 0), stop=(dc == NDC - 1))
                            for kc in range(NMC):
                                e.matmul(psF[pp][:, :], w_fg_ap(kc, mc),
                                         xbC[0][:, kc * CS:(kc + 1) * CS],
                                         start=(kc == 0), stop=(kc == NMC - 1))
                        df.step("pe", mmnf, mmg=True,
                                reads=[("ybufc", 0), ("xbc", 0)] +
                                      [("wa", dc) for dc in range(NDC)] +
                                      [("wa", 16 + kc) for kc in range(NMC)],
                                writes=[("psN", pp), ("psF", pp)])
                        mh = mc % 4
                        df.step("act", lambda e, mh=mh, pp=pp, mc=mc:
                                e.activation(
                                    out=fgb2[:, mh * CS:(mh + 1) * CS],
                                    in_=psF[pp][:, :],
                                    func=AF.Sigmoid, bias=fgbc[:, mc:mc + 1]),
                                reads=[("psF", pp), ("cvecs2",)],
                                writes=[("fgb", mh)])
                        df.step("act", lambda e, mh=mh, pp=pp: e.activation(
                            out=nfb[:, mh * CS:(mh + 1) * CS], in_=psN[pp][:, :],
                            func=AF.Copy),
                            reads=[("psN", pp)], writes=[("nf", mh)])
                        if mc % 4 == 3:
                            m0 = mc - 3

                            def gate(e, m0=m0, par=par, lpar=par):
                                sl = slice(m0 * CS, (m0 + 4) * CS)
                                L = ln1[lpar]
                                e.tensor_sub(L[:, sl], xsC[0][:, sl], nfb[:, :])
                                e.tensor_mul(L[:, sl], fgb2[:, :], L[:, sl])
                                e.tensor_add(L[:, sl], L[:, sl], nfb[:, :])
                                e.tensor_add(L[:, sl], L[:, sl], xsC[0][:, sl])
                            df.step("dve", gate,
                                    reads=[("xsc", 0)] +
                                          [("nf", j) for j in range(4)] +
                                          [("fgb", j) for j in range(4)],
                                    writes=[("ln1", par, m0 + j)
                                            for j in range(4)])
                    chunks.append(fmc)
                return chunks

            def tailA_chunks(b, sc):
                par = (b * (S // CS) + sc) % 2
                chunks = list(ln_stats_chunks(par))

                def c_apply1(par=par):
                    def apply1(e, lpar=par):
                        mr = mrep[lpar][:, :]
                        rr = rrep[lpar][:, :]
                        mrv = RawAP(mr.tensor, mr.offset,
                                    [[CS, 128], [0, 8], [1, CS]])
                        rrv = RawAP(rr.tensor, rr.offset,
                                    [[CS, 128], [0, 8], [1, CS]])
                        e.tensor_tensor(xln1[:, :], ln1[lpar][:, :], mrv,
                                        ALU.subtract)
                        e.tensor_tensor(xln1[:, :], xln1[:, :], rrv, ALU.mult)
                        e.tensor_copy(xlnb[:, :], xln1[:, :])
                    df.step("dve", apply1,
                            reads=[("ln1", par, mc) for mc in range(NMC)] +
                                  [("mrep", par), ("rrep", par)],
                            writes=[("xln1", mc) for mc in range(NMC)] +
                                   [("xlnb", mc) for mc in range(NMC)])
                chunks.append(c_apply1)

                def c_mml():
                    def mml(e):
                        for tc in range(CS // 128):
                            for kc in range(NMC):
                                e.matmul(psL[:, tc * 8:(tc + 1) * 8],
                                         xlnb[:, kc * CS + tc * 128:
                                              kc * CS + (tc + 1) * 128],
                                         w_wg_ap(kc), start=(kc == 0),
                                         stop=(kc == NMC - 1))
                    df.step("pe", mml, mmg=True,
                            reads=[("xlnb", kc) for kc in range(NMC)] +
                                  [("wa", 32)],
                            writes=[("psL",)])
                chunks.append(c_mml)

                def c_sm1():
                    def sm1(e):
                        bg4 = bgrep[:, :]
                        e.tensor_tensor(lgt[:, :], psL[:, :],
                                        RawAP(bg4.tensor, bg4.offset,
                                              [[8, 128], [0, 4], [1, 8]]),
                                        ALU.add)
                        for tc in range(CS // 128):
                            e.reduce_max(mxc[:, tc:tc + 1],
                                         lgt[:, tc * 8:(tc + 1) * 8],
                                         axis=mybir.AxisListType.X)
                        for tc in range(CS // 128):
                            e.tensor_scalar(lgt[:, tc * 8:(tc + 1) * 8],
                                            lgt[:, tc * 8:(tc + 1) * 8],
                                            mxc[:, tc:tc + 1], None,
                                            ALU.subtract)
                    df.step("dve", sm1, reads=[("psL",), ("cvecs2",)],
                            writes=[("lgt",), ("mxc",)])
                    df.step("act", lambda e: e.activation(
                        out=lgt[:, :], in_=lgt[:, :], func=AF.Exp),
                        reads=[("lgt",)], writes=[("lgt",)])
                chunks.append(c_sm1)

                def c_sm2():
                    def sm2(e):
                        for tc in range(CS // 128):
                            e.reduce_sum(smc[:, tc:tc + 1],
                                         lgt[:, tc * 8:(tc + 1) * 8],
                                         axis=mybir.AxisListType.X)
                        e.reciprocal(bpc[:, :], smc[:, :])
                    df.step("dve", sm2, reads=[("lgt",)],
                            writes=[("smc",)] + [("bpc", tc) for tc in range(4)])

                    def bp_tr(e):
                        for tc in range(CS // 128):
                            e.matmul(psS[0:1, tc * 128:(tc + 1) * 128],
                                     bpc[:, tc:tc + 1], identf[:, :],
                                     start=True, stop=True)
                    df.step("pe", bp_tr, mmg=True,
                            reads=[("bpc", tc) for tc in range(CS // 128)] +
                                  [("identf",)],
                            writes=[("psS",)])
                    df.step("dve", lambda e: e.tensor_copy(mrow[:, :],
                                                           psS[0:1, :]),
                            reads=[("psS",)], writes=[("mrow",)])
                    df.step("pe", lambda e: e.matmul(
                        psB[:, :], onesrow[:, :], mrow[:, :],
                        start=True, stop=True), mmg=True,
                        reads=[("mrow",), ("onesrow",)], writes=[("psB",)])
                    df.step("dve", lambda e: e.tensor_copy(bprep[:, :],
                                                           psB[:, :]),
                            reads=[("psB",)], writes=[("bprep",)])
                chunks.append(c_sm2)

                for m0 in range(0, NMC, 2):
                    def c_mme(m0=m0, par=par):
                        for mc in range(m0, m0 + 2):
                            pp = mc % 2

                            def mme(e, mc=mc, pp=pp):
                                for kc in range(NMC):
                                    e.matmul(psN[pp][:, :], w_we_ap(kc, mc),
                                             xlnb[:, kc * CS:(kc + 1) * CS],
                                             start=(kc == 0),
                                             stop=(kc == NMC - 1))
                            df.step("pe", mme, mmg=True,
                                    reads=[("xlnb", kc) for kc in range(NMC)] +
                                          [("wa", 24 + kc)
                                           for kc in range(NMC)],
                                    writes=[("psN", pp)])

                            def ffn_res(e, mc=mc, pp=pp, lpar=par):
                                sl = slice(mc * CS, (mc + 1) * CS)
                                e.scalar_tensor_tensor(ffnb[:, :],
                                                       psN[pp][:, :],
                                                       bec[:, mc:mc + 1],
                                                       bprep[:, :],
                                                       ALU.add, ALU.mult)
                                e.tensor_add(ln1[lpar][:, sl], xln1[:, sl],
                                             ffnb[:, :])
                            df.step("dve", ffn_res,
                                    reads=[("psN", pp), ("bprep",),
                                           ("xln1", mc), ("cvecs2",)],
                                    writes=[("ffnb",), ("ln1", par, mc)])
                    chunks.append(c_mme)
                return chunks

            def tailB_chunks(b, sc):
                s0 = sc * CS
                par = (b * (S // CS) + sc) % 2
                chunks = list(ln_stats_chunks(par))

                def c_out(b=b, s0=s0, par=par):
                    def apply2(e, lpar=par):
                        mr = mrep[lpar][:, :]
                        rr = rrep[lpar][:, :]
                        mrv = RawAP(mr.tensor, mr.offset,
                                    [[CS, 128], [0, 8], [1, CS]])
                        rrv = RawAP(rr.tensor, rr.offset,
                                    [[CS, 128], [0, 8], [1, CS]])
                        e.tensor_tensor(xln1[:, :], ln1[lpar][:, :], mrv,
                                        ALU.subtract)
                        e.tensor_tensor(xln1[:, :], xln1[:, :], rrv, ALU.mult)
                    df.step("dve", apply2,
                            reads=[("ln1", par, mc) for mc in range(NMC)] +
                                  [("mrep", par), ("rrep", par)],
                            writes=[("xln1", mc) for mc in range(NMC)])
                    of_base = out_fm[b, :, s0:s0 + CS]
                    xl_base = xln1[:, :]
                    df.dma([(RawAP(of_base.tensor, of_base.offset,
                                   [[S, 128], [128 * S, 8], [1, CS]]),
                             RawAP(xl_base.tensor, xl_base.offset,
                                   [[8 * CS, 128], [CS, 8], [1, CS]]))],
                           reads=[("xln1", mc) for mc in range(NMC)])
                chunks.append(c_out)
                return chunks

            its = [(b, sc) for b in range(BLOC) for sc in range(S // CS)]
            nit = len(its)
            def merge(fc, ta, tb):
                fc = list(fc or [])
                ta = list(ta or [])
                tb = list(tb or [])
                out = fc[:1]
                fseq = fc[1:]
                pre, post = fseq[:3], fseq[3:]
                i = j = 0
                while i < len(pre) or j < len(tb):
                    if i < len(pre):
                        out.append(pre[i]); i += 1
                    if j < len(tb):
                        out.append(tb[j]); j += 1
                i = j = 0
                while i < len(post) or j < len(ta):
                    if i < len(post):
                        out.append(post[i]); i += 1
                    if j < len(ta):
                        out.append(ta[j]); j += 1
                for ch in out:
                    ch()

            for k in range(nit + 2):
                fc = front_chunks(*its[k]) if k < nit else []
                ta = tailA_chunks(*its[k - 1]) if 1 <= k <= nit else []
                tb = tailB_chunks(*its[k - 2]) if 2 <= k <= nit + 1 else []
                merge(fc, ta, tb)
            df.flush(block)
        df.finish(block)
    return nc


_NC_CACHE = []


def kernel(L0_attn_x, L0_attn_in_proj_w, L0_attn_conv_w, L0_attn_conv_b,
           L0_attn_x_proj_w, L0_attn_dt_proj_w, L0_attn_dt_proj_b, L0_attn_A_log,
           L0_attn_D, L0_attn_out_proj_w, L0_attn_fg_w, L0_attn_fg_b,
           L0_ffn_Wg, L0_ffn_bg, L0_ffn_We, L0_ffn_be):
    x = np.asarray(L0_attn_x, dtype=np.float32)
    if not _NC_CACHE:
        _NC_CACHE.append(build_nc())
    nc = _NC_CACHE[0]

    shared = dict(
        w_in=np.ascontiguousarray(L0_attn_in_proj_w, dtype=np.float32),
        w_xp=np.ascontiguousarray(L0_attn_x_proj_w, dtype=np.float32),
        w_dt=np.ascontiguousarray(L0_attn_dt_proj_w, dtype=np.float32),
        w_out=np.ascontiguousarray(L0_attn_out_proj_w, dtype=np.float32),
        w_fg=np.ascontiguousarray(L0_attn_fg_w, dtype=np.float32),
        w_wg=np.ascontiguousarray(L0_ffn_Wg, dtype=np.float32),
        w_we=np.ascontiguousarray(L0_ffn_We, dtype=np.float32),
        conv_w=np.ascontiguousarray(
            np.asarray(L0_attn_conv_w, dtype=np.float32).reshape(DI, 4)),
        conv_b=np.ascontiguousarray(L0_attn_conv_b, dtype=np.float32),
        dt_b=np.ascontiguousarray(L0_attn_dt_proj_b, dtype=np.float32),
        a_log=np.ascontiguousarray(L0_attn_A_log, dtype=np.float32),
        d_vec=np.ascontiguousarray(L0_attn_D, dtype=np.float32),
        fg_b=np.ascontiguousarray(L0_attn_fg_b, dtype=np.float32),
        bg=np.ascontiguousarray(L0_ffn_bg, dtype=np.float32),
        be=np.ascontiguousarray(L0_ffn_be, dtype=np.float32),
    )
    in_maps = []
    for i in range(NCORE):
        m = dict(shared)
        m["x_fm"] = np.ascontiguousarray(
            x[i * BLOC:(i + 1) * BLOC].transpose(0, 2, 1))
        in_maps.append(m)

    res = run_bass_kernel_spmd(nc, in_maps, core_ids=list(range(NCORE)))
    out = np.concatenate(
        [np.asarray(r["out_fm"]).transpose(0, 2, 1) for r in res.results], axis=0)
    return out
